# version 21
# baseline (speedup 1.0000x reference)
import numpy as np
import ml_dtypes
from contextlib import ExitStack
import concourse.bass as bass
import concourse.mybir as mybir

F32 = mybir.dt.float32
BF16 = mybir.dt.bfloat16
U32 = mybir.dt.uint32
U8 = mybir.dt.uint8
AF = mybir.ActivationFunctionType
OP = mybir.AluOpType
AX = mybir.AxisListType

T, D, HALF = 4096, 1024, 2048
NCH = 16
IC = 8
E, HID = 8, 4096
NWE, HCE, NTB = 8, 4, 4
NG, GW, HD = 2, 512, 64
CAP = 1536
NSLOT = E * CAP
NST = CAP // 128
NTB3 = CAP // 512
SCALE = HD ** -0.5


def host_inputs(full, core):
    f32 = np.float32
    bf = ml_dtypes.bfloat16
    return {
        "x": np.ascontiguousarray(full["L0_attn_x"][core]),
        "wqT": np.ascontiguousarray(full["L0_attn_Wq"].T),
        "wkT": np.ascontiguousarray(full["L0_attn_Wk"].T),
        "wvT": np.ascontiguousarray(full["L0_attn_Wv"].T),
        "wbT": np.ascontiguousarray(full["L0_attn_Wbeta"].T),
        "waT": np.ascontiguousarray(full["L0_attn_Walpha"].T),
        "bk_r": np.ascontiguousarray(full["L0_attn_bk"].reshape(1, D)),
        "bq_r": np.ascontiguousarray(full["L0_attn_bq"].reshape(1, D)),
        "bv_r": np.ascontiguousarray(full["L0_attn_bv"].reshape(1, D)),
        "bb_r": np.ascontiguousarray(full["L0_attn_bbeta"].reshape(1, D)),
        "ba_r": np.ascontiguousarray(full["L0_attn_balpha"].reshape(1, D)),
        "wgred": full["L0_ffn_wg_red"],
        "wg": full["L0_ffn_wg"],
        "w1": full["L0_ffn_w1"],
        "b1_c": np.ascontiguousarray(
            full["L0_ffn_b1"].reshape(E, 32, 128).transpose(2, 0, 1).reshape(128, 256)),
        "w2": full["L0_ffn_w2"],
        "b2": full["L0_ffn_b2"],
        "ident32": np.eye(128, dtype=f32),
        "u64": np.ascontiguousarray((np.triu(np.ones((128, 128), f32)) / 64.0).astype(f32)),
        "u1bf": np.ascontiguousarray(np.triu(np.ones((128, 128), bf))),
        "u1x32": np.ascontiguousarray(np.triu(np.ones((32, 32), f32), k=1)),
        "onescol_bf": np.ones((128, 1), bf),
        "gbase": (np.arange(8, dtype=f32) * 1536.0).reshape(1, 8),
        "b2r": np.ascontiguousarray(full["L0_ffn_b2"].reshape(1, 8 * 1024)),
        "identbf": np.eye(128, dtype=bf),
        "ntriubig": np.ascontiguousarray(np.tile(
            ((1.0 - np.triu(np.ones((128, 128), f32))) * np.float32(1e9)).astype(bf), (1, 8))),
        "onehot8": np.ascontiguousarray(
            np.concatenate([np.eye(8, dtype=f32)[:, h:h + 1] * np.ones((1, 128), f32)
                            for h in range(8)], axis=1)),
        "ones_r": np.ones((1, 128), f32),
        "ones_bf": np.ones((1, 512), bf),
    }


class Ctr:
    def __init__(self, es, nc, name):
        self.sem = es.enter_context(nc.semaphore(name))
        self.v = 0
        self.name = name

    def inc(self, ins, n=1):
        ins.then_inc(self.sem, n)
        self.v += n
        return self.v


class Reg:
    def __init__(self, name):
        self.name = name
        self.w = None
        self.r = {}


def build(interleave=True, dbg=False):
    nc = bass.Bass("TRN2", target_bir_lowering=False)
    ap = {}

    def din(name, shape, dt=F32):
        ap[name] = nc.dram_tensor(name, list(shape), dt, kind="ExternalInput").ap()
        return ap[name]

    din("x", (T, D))
    for n in ("wqT", "wkT", "wvT", "wbT", "waT"):
        din(n, (D, D))
    din("bk_r", (1, D)); din("bq_r", (1, D))
    din("bv_r", (1, D)); din("bb_r", (1, D)); din("ba_r", (1, D))
    din("wgred", (16, D)); din("wg", (E, 16))
    din("w1", (E, D, HID)); din("b1_c", (128, 256))
    din("w2", (E, HID, D)); din("b2", (E, D))
    din("ident32", (128, 128))
    din("u64", (128, 128)); din("ntriubig", (128, 1024), BF16)
    din("u1bf", (128, 128), BF16); din("u1x32", (32, 32)); din("onescol_bf", (128, 1), BF16)
    din("gbase", (1, 8)); din("b2r", (1, 8 * 1024)); din("identbf", (128, 128), BF16)
    din("onehot8", (8, 8 * 128))
    din("ones_r", (1, 128)); din("ones_bf", (1, 512), BF16)

    out_d = nc.dram_tensor("out", [T, D], F32, kind="ExternalOutput").ap()
    xn_tok = nc.dram_tensor("xn_tok", [T, D], BF16, kind="Internal").ap()
    xn_perm = nc.dram_tensor("xn_perm", [NSLOT, D], BF16, kind="Internal").ap()
    ffn_perm = nc.dram_tensor("ffn_perm", [NSLOT, D], BF16, kind="Internal").ap()
    y_tok = nc.dram_tensor("y_tok", [T, D], BF16, kind="Internal").ap()

    with ExitStack() as es:
        sb = lambda name, shape, dt=F32: es.enter_context(nc.sbuf_tensor(name, list(shape), dt))
        sem = lambda n: es.enter_context(nc.semaphore(n))
        ct, cv, cs = Ctr(es, nc, "ct"), Ctr(es, nc, "cv"), Ctr(es, nc, "cs")
        te, ve, se, sy, gp = nc.tensor, nc.vector, nc.scalar, nc.sync, nc.gpsimd
        waited = {}

        def W(eng, ename, ctr, tok):
            if tok is None or tok <= 0:
                return
            key = (ename, ctr.name)
            if waited.get(key, 0) >= tok:
                return
            eng.wait_ge(ctr.sem, tok)
            waited[key] = tok

        def Ws(eng, ename, semh, sname, tok):
            if tok <= 0:
                return
            key = (ename, sname)
            if waited.get(key, 0) >= tok:
                return
            eng.wait_ge(semh, tok)
            waited[key] = tok

        def pe_begin(*rs):
            for reg in rs:
                for (ctr_, tok_) in reg.r.values():
                    W(te, "t", ctr_, tok_)
                reg.r = {}

        def pe_end(last_ins, *rs):
            tok = ct.inc(last_ins)
            for reg in rs:
                reg.w = (ct, tok)
            return tok

        def rd_begin(eng, ename, *rs):
            for reg in rs:
                if reg.w is not None:
                    W(eng, ename, reg.w[0], reg.w[1])

        def rd_end(ctr, tok, *rs):
            for reg in rs:
                reg.r[ctr.name] = (ctr, tok)

        score_sl = sb("score_sl", (128, 2 * NCH))
        idx_sl = sb("idx_sl", (128, 2 * NCH))
        pos_sl = sb("pos_sl", (128, 2 * NCH), U32)
        msk_sl = sb("msk_sl", (128, 2 * NCH * E), BF16)
        state = [sb(f"state{g}", (128, 512)) for g in range(NG)]
        state_bf = sb("state_bf", (128, 512), BF16)
        x_sb = [sb(f"x_sb{i}", (128, D)) for i in range(2)]
        out_sb = [sb(f"out_sb{i}", (128, D)) for i in range(2)]
        xn_sb = out_sb[0]
        xnT32 = out_sb[1]
        ident32 = sb("c_id32", (128, 128))
        identbf = sb("c_idbf", (128, 128), BF16)
        u64 = sb("c_u64", (128, 128))
        u1bf = sb("c_u1bf", (128, 128), BF16)
        u1x32 = sb("c_u1x32", (32, 32))
        onescol_bf = sb("c_ocbf", (128, 1), BF16)
        gbase_r = sb("c_gbase", (1, 8))
        ntriubig = sb("c_ntriu", (128, 1024), BF16)
        onehot8 = sb("c_oh8", (8, 8 * 128))
        ones_r = sb("c_ones", (1, 128))
        ones_bf = sb("c_onesbf", (1, 512), BF16)
        c_eps = sb("c_eps", (128, 1)); c_one = sb("c_one", (128, 1))
        brow4 = [sb(f"p_brow{i}", (1, GW), BF16) for i in range(4)]
        wred8 = sb("p_wred8", (128, IC * E))
        bnst = [sb(f"g_bnst{i}", (128, 12)) for i in range(2)]
        mv = [sb(f"g_mv{i}", (128, 2)) for i in range(2)]
        stdv = [sb(f"g_std{i}", (128, 1)) for i in range(2)]
        rstd = [sb(f"g_rstd{i}", (128, 1)) for i in range(2)]
        lg = [sb(f"g_lg{i}", (128, 8)) for i in range(2)]
        mx8 = [sb(f"g_mx8{i}", (128, 8)) for i in range(2)]
        ix8 = [sb(f"g_ix8{i}", (128, 8), U32) for i in range(2)]
        expd = [sb(f"g_expd{i}", (128, 8)) for i in range(2)]
        ngmx = [sb(f"g_ngmx{i}", (128, 1)) for i in range(2)]
        ssum = [sb(f"g_ssum{i}", (128, 1)) for i in range(2)]
        tot_sb = sb("pm_tot", (8, 32)); cb_sb = sb("pm_cb", (32, 8))
        rowsb = sb("pm_row", (1, 8)); asb = sb("pm_a", (128, 8)); absb = sb("pm_ab", (128, 8))
        t3sb = sb("pm_t3", (128, 8))
        rk_sb = sb("pm_rk", (128, 1)); ov_sb = sb("pm_ov", (128, 1)); posf = sb("pm_posf", (128, 1))

        ov_a = ExitStack()
        asb_ = lambda name, shape, dt=F32: ov_a.enter_context(nc.sbuf_tensor(name, list(shape), dt))
        xnT = asb_("xnT", (128, IC * HALF), BF16)
        slab = {n: asb_(f"at_{n}", (128, IC * GW), BF16)
                for n in ("wq", "wk", "wv", "wb", "wa")}
        q_sl = [asb_(f"q_sl{i}", (128, 2048), BF16) for i in range(2)]
        k_sl = [asb_(f"k_sl{i}", (128, 2048), BF16) for i in range(2)]
        v_sl4 = asb_("v_sl4", (128, 2048), BF16)
        L1b4 = asb_("L1b4", (128, 2048), BF16)
        L1a4 = asb_("L1a4", (128, 2048), BF16)
        vb_sl = [asb_(f"vb_sl{i}", (128, 2048), BF16) for i in range(2)]
        vbm_sl = [asb_(f"vbm_sl{i}", (128, 2048), BF16) for i in range(2)]
        ktok = [asb_(f"ktok{i}", (128, 2048), BF16) for i in range(2)]
        adT = [asb_(f"adT{i}", (128, 1024), BF16) for i in range(2)]
        browq = asb_("browq", (1, GW), BF16)
        browk = asb_("browk", (1, GW), BF16)
        CSb = asb_("CSb", (128, GW))
        NEG = asb_("NEG", (128, GW)); M2 = asb_("M2", (128, GW))
        BSP = asb_("BSP", (128, GW)); JUNK = asb_("JUNK", (128, GW))
        Dsb = asb_("Dsb", (128, 1024)); Dm_all = asb_("Dm_all", (128, 1024))
        dec_all = asb_("dec_all", (128, 1024), BF16)
        yi_sb = asb_("yi_sb", (128, GW)); t1_sb = asb_("t1_sb", (128, GW))
        tmc = asb_("tmc", (128, GW))
        lasum = [asb_(f"a_lasum{i}", (128, 32)) for i in range(2)]
        bmsum = asb_("a_bmsum", (128, 8))
        cums_b = [asb_(f"a_cums{i}", (128, 32)) for i in range(2)]
        cumsT_b = [asb_(f"a_cumsT{i}", (8, 512)) for i in range(2)]
        curve_b = [asb_(f"a_curve{i}", (128, 32)) for i in range(2)]
        asumb_b = [asb_(f"a_asumb{i}", (128, 32)) for i in range(2)]
        xnbf = [asb_(f"xnbf{i}", (128, D), BF16) for i in range(2)]
        ybf = [asb_(f"ybf{i}", (128, GW), BF16) for i in range(2)]
        st_ov = ExitStack()
        ssb = lambda name, shape, dt=F32: st_ov.enter_context(nc.sbuf_tensor(name, list(shape), dt))
        wg_sb = ssb("p_wg", (E, 16)); wgred_sb = ssb("p_wgred", (16, 512))
        wguT = ssb("p_wguT", (16, E)); wred8T = ssb("p_wred8T", (E, 512))
        wgsq = ssb("p_wgsq", (E, 16))
        nrm = ssb("p_nrm", (E, 1)); nrm2 = ssb("p_nrm2", (E, 1)); rinv = ssb("p_rinv", (E, 1))
        wgu = ssb("p_wgu", (E, 16))

        P0 = es.enter_context(nc.psum_tensor("P0", [128, 1024], F32))
        P1 = es.enter_context(nc.psum_tensor("P1", [128, 1024], F32))
        P2 = es.enter_context(nc.psum_tensor("P2", [128, 1024], F32))
        P3a = es.enter_context(nc.psum_tensor("P3a", [128, 512], F32))
        P8 = es.enter_context(nc.psum_tensor("P8", [128, 512], F32))
        R = {n: Reg(n) for n in ("P0a", "P0b", "P1a", "P1b", "P2a", "P2b", "P3a", "P8")}

        s_pre = sem("s_pre")
        s_x = [sem("s_x0"), sem("s_x1")]
        x_cnt = [0, 0]; x_rel = [None, None]
        s_w = [sem("s_w0"), sem("s_w1"), sem("s_w2")]
        s_fp = sem("s_fp")
        w_cnt = [0, 0, 0]; w_rel = [None, None, None]; fp_cnt = [0]
        s_wa = sem("s_wa")
        wa_cnt = [0]; wa_rel = [None]
        s_o = [sem("s_o0"), sem("s_o1")]
        o_cnt = [0, 0]; o_rel = [None, None]
        s_t = [sem("s_t0"), sem("s_t1")]
        t_cnt = [0, 0]
        s_r = [sem("s_r0"), sem("s_r1"), sem("s_r2"), sem("s_r3")]
        r_cnt = [0, 0, 0, 0]
        s_g = [sem("s_g0"), sem("s_g1")]
        g_cnt = [0, 0]
        bc_reg = es.enter_context(gp.register("bc"))
        gp.reg_mov(bc_reg, NSLOT - 1)

        def barrier(engines=("t", "v", "s", "sy", "gp")):
            objs = {"t": te, "v": ve, "s": se, "sy": sy, "gp": gp}
            for en in engines:
                eng = objs[en]
                for ctr in (ct, cv, cs):
                    W(eng, en, ctr, ctr.v)
                for p_ in range(2):
                    Ws(eng, en, s_o[p_], f"s_o{p_}", 16 * o_cnt[p_])
                    Ws(eng, en, s_t[p_], f"s_t{p_}", 16 * t_cnt[p_])
                    Ws(eng, en, s_r[p_], f"s_r{p_}", 16 * r_cnt[p_])
                    Ws(eng, en, s_g[p_], f"s_g{p_}", 16 * g_cnt[p_])
                    Ws(eng, en, s_x[p_], f"s_x{p_}", 16 * x_cnt[p_])
                for p_ in range(3):
                    Ws(eng, en, s_w[p_], f"s_w{p_}", 16 * w_cnt[p_])
                Ws(eng, en, s_wa, "s_wa", 16 * wa_cnt[0])
                Ws(eng, en, s_fp, "s_fp", 16 * fp_cnt[0])

        def x_load(par, row0):
            if x_rel[par] is not None:
                W(sy, "sy", x_rel[par][0], x_rel[par][1])
            sy.dma_start(x_sb[par][:, :], ap["x"][row0:row0 + 128, :]).then_inc(s_x[par], 16)
            x_cnt[par] += 1

        def x_wait(eng, ename, par):
            Ws(eng, ename, s_x[par], f"s_x{par}", 16 * x_cnt[par])

        npre = 0
        for dst, srcn in ((ident32, "ident32"), (identbf, "identbf"), (u64, "u64"),
                          (u1bf, "u1bf"), (u1x32, "u1x32"), (onescol_bf, "onescol_bf"),
                          (gbase_r, "gbase"),
                          (ntriubig, "ntriubig"), (onehot8, "onehot8"),
                          (ones_r, "ones_r"), (ones_bf, "ones_bf"),
                          (wg_sb, "wg")):
            sy.dma_start(dst[:, :], ap[srcn][:, :]).then_inc(s_pre, 16)
            npre += 16



        Ws(se, "s", s_pre, "s_pre", npre)
        Ws(ve, "v", s_pre, "s_pre", npre)
        Ws(te, "t", s_pre, "s_pre", npre)
        tok = cs.inc(se.activation(wgsq[:, :], wg_sb[:, :], AF.Square, accum_out=nrm2[:, :]))
        W(se, "s", cs, tok)
        tok = cs.inc(se.activation(nrm[:, :], nrm2[:, :], AF.Sqrt))
        W(ve, "v", cs, tok)
        tok = cv.inc(ve.tensor_scalar_max(nrm[:, :], nrm[:, :], 1e-4))
        W(ve, "v", cv, tok)
        tok = cv.inc(ve.reciprocal(rinv[:, :], nrm[:, :]))
        W(ve, "v", cv, tok)
        tok_wgu = cv.inc(ve.tensor_scalar_mul(wgu[:, :], wg_sb[:, :], rinv[:, :]))
        for g in range(NG):
            cv.inc(ve.memset(state[g][:, :], 0.0))
        tok = cv.inc(ve.memset(c_eps[:, :], 1e-5))
        tok_c1 = cv.inc(ve.memset(c_one[:, :], 1.0))
        W(se, "s", cv, tok_c1)
        W(te, "t", cv, tok_wgu)
        pe_begin(R["P3a"])
        pe_end(te.transpose(P3a[0:16, 0:8], wgu[:, :], ident32[0:8, 0:8]), R["P3a"])
        rd_begin(ve, "v", R["P3a"])
        tok = cv.inc(ve.tensor_copy(wguT[:, :], P3a[0:16, 0:8]))
        rd_end(cv, tok, R["P3a"])
        for half_c in range(2):
            W(sy, "sy", ct, ct.v)
            sy.dma_start(wgred_sb[:, :],
                         ap["wgred"][:, half_c * 512:(half_c + 1) * 512]).then_inc(s_pre, 16)
            npre += 16
            Ws(te, "t", s_pre, "s_pre", npre)
            W(te, "t", cv, tok)
            pe_begin(R["P3a"])
            pe_end(te.matmul(P3a[0:8, 0:512], wguT[:, :], wgred_sb[:, :],
                             start=True, stop=True), R["P3a"])
            rd_begin(ve, "v", R["P3a"])
            tok = cv.inc(ve.tensor_copy(wred8T[:, :], P3a[0:8, 0:512]))
            rd_end(cv, tok, R["P3a"])
            for ic4 in range(4):
                ic = half_c * 4 + ic4
                W(te, "t", cv, tok)
                pe_begin(R["P3a"])
                pe_end(te.transpose(P3a[:, 0:8], wred8T[:, ic4 * 128:(ic4 + 1) * 128],
                                    ident32[0:8, 0:8]), R["P3a"])
                rd_begin(ve, "v", R["P3a"])
                tok = cv.inc(ve.tensor_copy(wred8[:, ic * 8:(ic + 1) * 8], P3a[:, 0:8]))
                rd_end(cv, tok, R["P3a"])
        st_ov.close()

        SC_T = 512
        NSC = HALF // SC_T

        def attn_phase(half):
            hb = half * HALF
            slab_rel = [0, 0]
            adT_rel = [0, 0]
            mem_tok = [0]
            exp_rel = [0]
            admul_rel = [0]
            curve_tok = [0, 0]
            ct2_tok = [0, 0]
            proj_cs = [0, 0]; proj_cv = [0, 0]
            lasum_cs = [0, 0]; lasum_cv = [0, 0]
            sbf_tok = [0]
            for grp in range(NG):
                gofs = grp * GW
                if wa_rel[0] is not None:
                    W(gp, "gp", wa_rel[0][0], wa_rel[0][1])
                for n, wn in (("wq", "wqT"), ("wk", "wkT"), ("wv", "wvT"),
                              ("wb", "wbT"), ("wa", "waT")):
                    gp.dma_start(
                        slab[n][:, :].rearrange("p (i o) -> p i o", o=GW),
                        ap[wn][:, gofs:gofs + GW].rearrange("(i p) o -> p i o", p=128)
                    ).then_inc(s_wa, 16)
                    wa_cnt[0] += 1
                for i, bn in enumerate(("bv_r", "bb_r", "ba_r", "bk_r")):
                    gp.dma_start(brow4[i][:, :], ap[bn][0:1, gofs:gofs + GW]).then_inc(s_wa, 16)
                    wa_cnt[0] += 1
                gp.dma_start(browq[:, :], ap["bq_r"][0:1, gofs:gofs + GW]).then_inc(s_wa, 16)
                gp.dma_start(browk[:, :], ap["bk_r"][0:1, gofs:gofs + GW]).then_inc(s_wa, 16)
                wa_cnt[0] += 2
                Ws(te, "t", s_wa, "s_wa", 16 * wa_cnt[0])

                pseq = [0]
                W(ve, "v", ct, mem_tok[0])
                W(ve, "v", cv, cv.v)
                sbf_tok[0] = cv.inc(ve.tensor_copy(state_bf[:, :], state[grp][:, :]))

                def proj_gen(sc):
                    p = sc % 2
                    W(se, "s", ct, slab_rel[p])
                    W(ve, "v", ct, slab_rel[p])
                    for wname, dstq, brq in (("wq", q_sl[p], browq), ("wk", k_sl[p], browk)):
                        for oc in range(4):
                            reg = R["P0a"] if pseq[0] % 2 == 0 else R["P0b"]
                            cof = 0 if pseq[0] % 2 == 0 else 512
                            pseq[0] += 1
                            pe_begin(reg)
                            for ic in range(IC):
                                te.matmul(
                                    P0[:, cof:cof + 512],
                                    slab[wname][:, ic * GW + oc * 128: ic * GW + (oc + 1) * 128],
                                    xnT[:, ic * HALF + sc * SC_T: ic * HALF + (sc + 1) * SC_T],
                                    start=(ic == 0), stop=False)
                            last = te.matmul(P0[:, cof:cof + 512],
                                             brq[0:1, oc * 128:(oc + 1) * 128],
                                             ones_bf[0:1, 0:512], start=False, stop=True)
                            pe_end(last, reg)
                            rd_begin(se, "s", reg)
                            W(se, "s", cv, cv.v)
                            tok = cs.inc(se.mul(dstq[:, oc * 512:(oc + 1) * 512],
                                                P0[:, cof:cof + 512], SCALE))
                            rd_end(cs, tok, reg)
                            yield
                    for tc2 in range(4):
                        for wi, wname in enumerate(("wv", "wb", "wa", "wk")):
                            reg = R["P0a"] if pseq[0] % 2 == 0 else R["P0b"]
                            cof = 0 if pseq[0] % 2 == 0 else 512
                            pseq[0] += 1
                            pe_begin(reg)
                            for ic in range(IC):
                                te.matmul(
                                    P0[:, cof:cof + 512],
                                    xnT[:, ic * HALF + sc * SC_T + tc2 * 128:
                                        ic * HALF + sc * SC_T + (tc2 + 1) * 128],
                                    slab[wname][:, ic * GW:(ic + 1) * GW],
                                    start=(ic == 0), stop=False)
                            brow_i = brow4[3] if wi == 3 else brow4[wi]
                            last = te.matmul(P0[:, cof:cof + 512], ones_bf[0:1, 0:128],
                                             brow_i[:, :], start=False, stop=True)
                            pe_end(last, reg)
                            rd_begin(se, "s", reg)
                            W(se, "s", cv, cv.v)
                            if wi == 0:
                                tok = cs.inc(se.copy(v_sl4[:, tc2 * 512:(tc2 + 1) * 512],
                                                     P0[:, cof:cof + 512]))
                            elif wi == 3:
                                tok = cs.inc(se.mul(ktok[p][:, tc2 * GW:(tc2 + 1) * GW],
                                                    P0[:, cof:cof + 512], SCALE))
                            elif wi == 1:
                                tok = cs.inc(se.copy(L1b4[:, tc2 * 512:(tc2 + 1) * 512],
                                                     P0[:, cof:cof + 512]))
                            else:
                                tok = cs.inc(se.copy(L1a4[:, tc2 * 512:(tc2 + 1) * 512],
                                                     P0[:, cof:cof + 512]))
                            rd_end(cs, tok, reg)
                            yield
                    W(ve, "v", cs, cs.v)
                    for tc2 in range(4):
                        L1v = L1a4[:, tc2 * 512:(tc2 + 1) * 512]
                        W(ve, "v", cs, cs.v)
                        W(ve, "v", cv, cv.v)
                        tok = cv.inc(ve.tensor_scalar(CSb[:, :], L1v, 10.0, -10.0,
                                                      OP.min, OP.max))
                        W(ve, "v", cv, tok)
                        tok = cv.inc(ve.tensor_scalar_mul(NEG[:, :], CSb[:, :], -1.0))
                        W(ve, "v", cv, tok)
                        tok = cv.inc(ve.scalar_tensor_tensor(
                            M2[:, :], CSb[:, :], -1.0, CSb[:, :], OP.mult, OP.min))
                        W(se, "s", cv, tok)
                        tok = cs.inc(se.activation(M2[:, :], M2[:, :], AF.Exp))
                        W(se, "s", cs, tok)
                        tok_ln = cs.inc(se.activation(M2[:, :], M2[:, :], AF.Ln,
                                                      bias=c_one[:, :]))
                        W(ve, "v", cs, tok_ln)
                        for h in range(8):
                            cv.inc(ve.scalar_tensor_tensor(
                                JUNK[:, h * 64:(h + 1) * 64], NEG[:, h * 64:(h + 1) * 64],
                                0.0, M2[:, h * 64:(h + 1) * 64], OP.max, OP.add,
                                accum_out=lasum[p][:, tc2 * 8 + h: tc2 * 8 + h + 1]))
                        yield
                    lasum_cs[p] = cs.v
                    lasum_cv[p] = cv.v
                    for tc2 in range(4):
                        L1v = L1b4[:, tc2 * 512:(tc2 + 1) * 512]
                        vsv = v_sl4[:, tc2 * 512:(tc2 + 1) * 512]
                        W(ve, "v", cs, cs.v)
                        W(ve, "v", cv, cv.v)
                        tok = cv.inc(ve.scalar_tensor_tensor(
                            M2[:, :], L1v, -1.0, L1v, OP.mult, OP.min))
                        W(se, "s", cv, tok)
                        tok = cs.inc(se.activation(M2[:, :], M2[:, :], AF.Exp))
                        W(se, "s", cs, tok)
                        tok_ln = cs.inc(se.activation(M2[:, :], M2[:, :], AF.Ln,
                                                      bias=c_one[:, :]))
                        W(ve, "v", cs, tok_ln)
                        for h in range(8):
                            tok_sp = cv.inc(ve.scalar_tensor_tensor(
                                BSP[:, h * 64:(h + 1) * 64], L1v[:, h * 64:(h + 1) * 64],
                                0.0, M2[:, h * 64:(h + 1) * 64], OP.max, OP.add,
                                accum_out=bmsum[:, h:h + 1]))
                        W(ve, "v", cv, tok_sp)
                        tok_vb = cv.inc(ve.tensor_mul(
                            vb_sl[p][:, tc2 * GW:(tc2 + 1) * GW],
                            vsv, BSP[:, :]))
                        vview = vsv.rearrange("p (h d) -> p h d", d=64)
                        mview = vbm_sl[p][:, tc2 * GW:(tc2 + 1) * GW].rearrange(
                            "p (h d) -> p h d", d=64)
                        bmv = bmsum[:, :].rearrange("p (h d) -> p h d", d=1)
                        b0, b1 = bass.broadcast_tensor_aps(vview, bmv)
                        tok = cv.inc(ve.scalar_tensor_tensor(
                            mview, b0, 1.0 / 64, b1, OP.mult, OP.mult))
                        yield
                    proj_cs[p] = cs.v
                    proj_cv[p] = cv.v

                def smalls(sc):
                    p = sc % 2
                    W(te, "t", cv, lasum_cv[p])
                    W(te, "t", cs, lasum_cs[p])
                    pe_begin(R["P8"])
                    last = None
                    for c2 in range(4):
                        last = te.matmul(P8[:, 384 + c2 * 8:392 + c2 * 8], u64[:, :],
                                         lasum[p][:, c2 * 8:(c2 + 1) * 8],
                                         start=True, stop=True)
                    pe_end(last, R["P8"])
                    pe_begin(R["P1a"], R["P1b"])
                    last = None
                    for c2 in range(4):
                        last = te.matmul(P1[0:8, c2 * 128:(c2 + 1) * 128],
                                         lasum[p][:, c2 * 8:(c2 + 1) * 8],
                                         u64[:, :], start=True, stop=True)
                    pe_end(last, R["P1a"], R["P1b"])
                    rd_begin(ve, "v", R["P8"], R["P1a"], R["P1b"])
                    tok = cv.inc(ve.tensor_copy(cums_b[p][:, :], P8[:, 384:416]))
                    tok_ct2 = cv.inc(ve.tensor_copy(cumsT_b[p][:, :], P1[0:8, 0:512]))
                    rd_end(cv, tok_ct2, R["P8"], R["P1a"], R["P1b"])
                    ct2_tok[p] = tok_ct2
                    W(se, "s", cv, tok_ct2)
                    W(se, "s", cv, cv.v)
                    curve_tok[p] = cs.inc(se.activation(
                        curve_b[p][:, :], cums_b[p][:, :], AF.Exp, scale=-1.0))
                    W(te, "t", cv, tok_ct2)
                    pe_begin(R["P8"])
                    last = None
                    for c2 in range(4):
                        last = te.matmul(P8[0:1, 416 + c2 * 8:424 + c2 * 8],
                                         cumsT_b[p][:, c2 * 128 + 127: c2 * 128 + 128],
                                         ident32[0:8, 0:8], start=True, stop=True)
                    pe_end(last, R["P8"])
                    rd_begin(se, "s", R["P8"])
                    tok_as = cs.inc(se.activation(JUNK[0:1, 0:32], P8[0:1, 416:448],
                                                  AF.Exp, scale=-1.0))
                    rd_end(cs, tok_as, R["P8"])
                    W(te, "t", cs, tok_as)
                    pe_begin(R["P8"])
                    last = None
                    for c2 in range(4):
                        last = te.matmul(P8[:, 448 + c2 * 8:456 + c2 * 8], ones_r[:, :],
                                         JUNK[0:1, c2 * 8:(c2 + 1) * 8],
                                         start=True, stop=True)
                    pe_end(last, R["P8"])
                    rd_begin(ve, "v", R["P8"])
                    tok = cv.inc(ve.tensor_copy(asumb_b[p][:, :], P8[:, 448:480]))
                    rd_end(cv, tok, R["P8"])

                def prep(k):
                    sc, c2 = divmod(k, 4)
                    p = sc % 2
                    W(te, "t", cv, ct2_tok[p])
                    W(te, "t", cs, proj_cs[p])
                    W(te, "t", cv, proj_cv[p])
                    pe_begin(R["P1a"], R["P1b"])
                    last = None
                    for h in range(8):
                        last = te.matmul(P1[:, h * 128:(h + 1) * 128],
                                         onehot8[:, h * 128:(h + 1) * 128],
                                         cumsT_b[p][:, c2 * 128:(c2 + 1) * 128],
                                         start=True, stop=True)
                    pe_end(last, R["P1a"], R["P1b"])
                    pe_begin(R["P2a"], R["P2b"])
                    last = None
                    for h in range(8):
                        oc, poff = h // 2, (h % 2) * 64
                        dst = P2[:, (h // 2) * 128:(h // 2 + 1) * 128] if h % 2 == 0 \
                            else P2[:, 512 + (h // 2) * 128: 512 + (h // 2 + 1) * 128]
                        last = te.matmul(
                            dst,
                            k_sl[p][poff:poff + 64, oc * 512 + c2 * 128: oc * 512 + (c2 + 1) * 128],
                            q_sl[p][poff:poff + 64, oc * 512 + c2 * 128: oc * 512 + (c2 + 1) * 128],
                            start=True, stop=True)
                    pe_end(last, R["P2a"], R["P2b"])
                    rd_begin(ve, "v", R["P1a"], R["P1b"])
                    bview = P1[:, :].rearrange("p (h i) -> p h i", i=128)
                    cview = cums_b[p][:, c2 * 8:(c2 + 1) * 8].rearrange("p (h i) -> p h i", i=1)
                    b0, b1 = bass.broadcast_tensor_aps(bview, cview)
                    tok = cv.inc(ve.tensor_tensor(
                        Dsb[:, :].rearrange("p (h i) -> p h i", i=128), b0, b1, OP.subtract))
                    rd_end(cv, tok, R["P1a"], R["P1b"])
                    W(ve, "v", cv, tok)
                    W(ve, "v", cs, exp_rel[0])
                    tok_dm = cv.inc(ve.tensor_tensor(Dm_all[:, :], Dsb[:, :],
                                                     ntriubig[:, :], OP.add))
                    W(se, "s", cv, tok_dm)
                    W(se, "s", cv, admul_rel[0])
                    tok_dc = cs.inc(se.activation(dec_all[:, :], Dm_all[:, :], AF.Exp,
                                                  scale=-1.0))
                    exp_rel[0] = tok_dc
                    W(ve, "v", cs, tok_dc)
                    W(ve, "v", ct, adT_rel[k % 2])
                    rd_begin(ve, "v", R["P2a"], R["P2b"])
                    tok_ad = None
                    for par2 in range(2):
                        srcp = P2[:, 0:512] if par2 == 0 else P2[:, 512:1024]
                        aview = srcp.rearrange("p (a o i) -> p a o i", o=1, i=128)
                        dview = dec_all[:, :].rearrange("p (a b i) -> p a b i", b=2, i=128)[
                            :, :, par2:par2 + 1, :]
                        oview = adT[k % 2][:, :].rearrange("p (a b i) -> p a b i", b=2, i=128)[
                            :, :, par2:par2 + 1, :]
                        tok_ad = cv.inc(ve.tensor_tensor(oview, aview, dview, OP.mult))
                    admul_rel[0] = tok_ad
                    rd_end(cv, tok_ad, R["P2a"], R["P2b"])

                def scan(k):
                    sc, c2 = divmod(k, 4)
                    p = sc % 2
                    gi = k
                    W(te, "t", cv, sbf_tok[0])
                    pe_begin(R["P3a"])
                    last = None
                    for h in range(8):
                        oc = h // 2
                        last = te.matmul(
                            P3a[:, h * 64:(h + 1) * 64],
                            q_sl[p][:, oc * 512 + c2 * 128: oc * 512 + (c2 + 1) * 128],
                            state_bf[:, h * 64:(h + 1) * 64], start=True, stop=True)
                    tok_mm = pe_end(last, R["P3a"])
                    mem_tok[0] = tok_mm
                    W(te, "t", cv, admul_rel[0])
                    pe_begin(R["P0a"])
                    last = None
                    for h in range(8):
                        last = te.matmul(
                            P0[:, h * 64:(h + 1) * 64],
                            adT[k % 2][:, h * 128:(h + 1) * 128],
                            vb_sl[p][:, c2 * GW + h * 64: c2 * GW + (h + 1) * 64],
                            start=True, stop=True)
                    tok_yi_mm = pe_end(last, R["P0a"])
                    adT_rel[k % 2] = tok_yi_mm
                    pe_begin(R["P8"])
                    last = None
                    for h in range(8):
                        poff = (h % 2) * 64
                        last = te.matmul(
                            P8[poff:poff + 64, (h // 2) * 64:(h // 2 + 1) * 64],
                            ktok[p][:, c2 * GW + h * 64: c2 * GW + (h + 1) * 64],
                            vbm_sl[p][:, c2 * GW + h * 64: c2 * GW + (h + 1) * 64],
                            start=True, stop=True)
                    tok_dl = pe_end(last, R["P8"])
                    slab_rel[p] = tok_dl
                    rd_begin(ve, "v", R["P8"])
                    W(ve, "v", ct, mem_tok[0])
                    W(ve, "v", cv, cv.v)
                    tok_su = None
                    for h in range(8):
                        poff = (h % 2) * 64
                        cv.inc(ve.scalar_tensor_tensor(
                            state_bf[poff:poff + 64, h * 64:(h + 1) * 64],
                            state[grp][poff:poff + 64, h * 64:(h + 1) * 64],
                            asumb_b[p][poff:poff + 64, c2 * 8 + h: c2 * 8 + h + 1],
                            P8[poff:poff + 64, (h // 2) * 64:(h // 2 + 1) * 64],
                            OP.mult, OP.add))
                    for h in range(8):
                        poff = (h % 2) * 64
                        tok_su = cv.inc(ve.scalar_tensor_tensor(
                            state[grp][poff:poff + 64, h * 64:(h + 1) * 64],
                            state[grp][poff:poff + 64, h * 64:(h + 1) * 64],
                            asumb_b[p][poff:poff + 64, c2 * 8 + h: c2 * 8 + h + 1],
                            P8[poff:poff + 64, (h // 2) * 64:(h // 2 + 1) * 64],
                            OP.mult, OP.add))
                    sbf_tok[0] = tok_su
                    rd_end(cv, tok_su, R["P8"])
                    rd_begin(se, "s", R["P0a"])
                    W(se, "s", cv, cv.v)
                    tok_yi = cs.inc(se.copy(yi_sb[:, :], P0[:, 0:512]))
                    rd_end(cs, tok_yi, R["P0a"])
                    W(ve, "v", cs, tok_yi)
                    W(ve, "v", cs, curve_tok[p])
                    rd_begin(ve, "v", R["P3a"])
                    mview = P3a[:, :].rearrange("p (h d) -> p h d", d=64)
                    cuv = curve_b[p][:, c2 * 8:(c2 + 1) * 8].rearrange("p (h d) -> p h d", d=1)
                    m0, m1 = bass.broadcast_tensor_aps(mview, cuv)
                    tok = cv.inc(ve.tensor_tensor(
                        tmc[:, :].rearrange("p (h d) -> p h d", d=64), m0, m1, OP.mult))
                    W(ve, "v", cv, tok)
                    Ws(ve, "v", s_t[c2 % 2], f"s_t{c2 % 2}", 16 * t_cnt[c2 % 2])
                    tok_ep = cv.inc(ve.tensor_add(ybf[c2 % 2][:, :], tmc[:, :], yi_sb[:, :]))
                    rd_end(cv, tok_ep, R["P3a"])
                    W(sy, "sy", cv, tok_ep)
                    sy.dma_start(y_tok[hb + gi * 128: hb + (gi + 1) * 128, gofs:gofs + GW],
                                 ybf[c2 % 2][:, :]).then_inc(s_t[c2 % 2], 16)
                    t_cnt[c2 % 2] += 1

                for _ in proj_gen(0):
                    pass
                smalls(0)
                for sc in range(NSC):
                    pg = proj_gen(sc + 1) if sc < NSC - 1 else iter(())

                    def pump(n):
                        for _ in range(n):
                            if next(pg, "D") == "D":
                                return

                    prep(4 * sc)
                    pump(3)
                    prep(4 * sc + 1)
                    pump(3)
                    scan(4 * sc)
                    pump(3)
                    prep(4 * sc + 2)
                    pump(3)
                    scan(4 * sc + 1)
                    pump(3)
                    prep(4 * sc + 3)
                    pump(3)
                    scan(4 * sc + 2)
                    pump(40)
                    scan(4 * sc + 3)
                    if sc < NSC - 1:
                        smalls(sc + 1)
                wa_rel[0] = (ct, ct.v)

        for half in range(2):
            hb = half * HALF
            barrier()
            x_load(0, hb)
            x_load(1, hb + 128)
            ln_tr_tok = [0, 0]
            ln_xnt_tok = [0, 0]
            for tc in range(NCH):
                par = tc % 2
                gc = half * NCH + tc
                xn_t = out_sb[par]
                x32 = Dsb if par == 0 else Dm_all
                x_wait(ve, "v", par)
                tok = cv.inc(ve.bn_stats(bnst[par][:, 0:6], x_sb[par][:, 0:512]))
                tok = cv.inc(ve.bn_stats(bnst[par][:, 6:12], x_sb[par][:, 512:1024]))
                W(ve, "v", cv, tok)
                tok = cv.inc(ve.bn_aggr(mv[par][:, :], bnst[par][:, :]))
                W(se, "s", cv, tok)
                tok_s = cs.inc(se.activation(stdv[par][:, :], mv[par][:, 1:2], AF.Sqrt,
                                             bias=c_eps[:, :]))
                W(ve, "v", cs, tok_s)
                tok = cv.inc(ve.reciprocal(rstd[par][:, :], stdv[par][:, :]))
                W(ve, "v", cv, tok)
                W(ve, "v", ct, ln_tr_tok[par])
                if o_rel[par] is not None:
                    W(ve, "v", o_rel[par][0], o_rel[par][1])
                Ws(ve, "v", s_o[par], f"s_o{par}", 16 * o_cnt[par])
                tok_xn = cv.inc(ve.tensor_scalar(xn_t[:, :], x_sb[par][:, :],
                                                 mv[par][:, 0:1], rstd[par][:, :],
                                                 OP.subtract, OP.mult))
                x_rel[par] = (cv, tok_xn)
                if tc + 2 < NCH:
                    x_load(par, hb + (tc + 2) * 128)
                W(te, "t", cv, tok_xn)
                regs_ = (R["P0a"], R["P0b"]) if par == 0 else (R["P2a"], R["P2b"])
                PT = P0 if par == 0 else P2
                pe_begin(*regs_)
                last = None
                for ic in range(IC):
                    last = te.transpose(PT[:, ic * 128:(ic + 1) * 128],
                                        xn_t[:, ic * 128:(ic + 1) * 128], ident32[:, :])
                pe_end(last, *regs_)
                rd_begin(ve, "v", *regs_)
                W(ve, "v", cv, ln_xnt_tok[par])
                tok = cv.inc(ve.tensor_copy(x32[:, 0:512], PT[:, 0:512]))
                tok32 = cv.inc(ve.tensor_copy(x32[:, 512:1024], PT[:, 512:1024]))
                rd_end(cv, tok32, *regs_)
                W(ve, "v", cv, tok32)
                tok_xnt = cv.inc(ve.tensor_copy(
                    xnT[:, :].rearrange("p (i t) -> p i t", t=HALF)[:, :, tc * 128:(tc + 1) * 128],
                    x32[:, :].rearrange("p (i t) -> p i t", t=128)))
                Ws(ve, "v", s_t[par], f"s_t{par}", 16 * t_cnt[par])
                tok_xb = cv.inc(ve.tensor_copy(xnbf[par][:, :], xn_t[:, :]))
                W(sy, "sy", cv, tok_xb)
                sy.dma_start(xn_tok[hb + tc * 128: hb + (tc + 1) * 128, :],
                             xnbf[par][:, :]).then_inc(s_t[par], 16)
                t_cnt[par] += 1
                W(te, "t", cv, tok32)
                greg = R["P3a"] if par == 0 else R["P8"]
                GP = P3a if par == 0 else P8
                pe_begin(greg)
                last = None
                for ic in range(IC):
                    last = te.matmul(GP[:, 0:8], x32[:, ic * 128:(ic + 1) * 128],
                                     wred8[:, ic * 8:(ic + 1) * 8],
                                     start=(ic == 0), stop=(ic == IC - 1))
                tok_g = pe_end(last, greg)
                ln_tr_tok[par] = tok_g
                rd_begin(ve, "v", greg)
                tok = cv.inc(ve.tensor_copy(lg[par][:, :], GP[:, 0:8]))
                rd_end(cv, tok, greg)
                ln_xnt_tok[par] = tok if False else ln_xnt_tok[par]
                W(ve, "v", cv, tok)
                tok = cv.inc(ve.reduce_max(ngmx[par][:, :], lg[par][:, :], axis=AX.X,
                                           negate=True))
                W(se, "s", cv, tok)
                tok_e = cs.inc(se.activation(expd[par][:, :], lg[par][:, :], AF.Exp,
                                             bias=ngmx[par][:, :]))
                W(ve, "v", cs, tok_e)
                tok = cv.inc(ve.reduce_sum(ssum[par][:, :], expd[par][:, :], axis=AX.X))
                W(ve, "v", cv, tok)
                tok = cv.inc(ve.reciprocal(score_sl[:, gc:gc + 1], ssum[par][:, :]))
                tok = cv.inc(ve.max(mx8[par][:, :], lg[par][:, :]))
                W(ve, "v", cv, tok)
                tok = cv.inc(ve.max_index(ix8[par][:, :], mx8[par][:, :], lg[par][:, :]))
                W(ve, "v", cv, tok)
                tok = cv.inc(ve.tensor_copy(idx_sl[:, gc:gc + 1], ix8[par][:, 0:1]))
                W(ve, "v", cv, tok)
                for e in range(E):
                    tok_m = cv.inc(ve.tensor_single_scalar(
                        msk_sl[:, gc * E + e:gc * E + e + 1], idx_sl[:, gc:gc + 1],
                        float(e), OP.is_equal))
                ln_xnt_tok[par] = tok_m

            attn_phase(half)

        barrier()
        ov_a.close()
        ov_b = ExitStack()
        bsb = lambda name, shape, dt=F32: ov_b.enter_context(nc.sbuf_tensor(name, list(shape), dt))
        w1e = [bsb(f"w1e{i}", (128, IC * 512), BF16) for i in range(3)]
        w2e = [bsb(f"w2e{i}", (128, HCE * 1024), BF16) for i in range(3)]
        h_sb = [bsb(f"h_sb{i}", (128, HCE * 512), BF16) for i in range(2)]
        b1c = bsb("p_b1c", (128, 256))
        b2all = bsb("p_b2all", (1, E * D), BF16)
        xnT_pe = [bsb(f"xnT_pe{i}", (128, IC * CAP), BF16) for i in range(2)]
        ffn_acc = bsb("ffn_acc", (128, NST * D), BF16)
        xperm_t = [bsb(f"xperm_t{i}", (128, D), BF16) for i in range(4)]
        ybuf = [bsb(f"ybuf{i}", (128, D), BF16) for i in range(2)]
        fbuf = [bsb(f"fbuf{i}", (128, D), BF16) for i in range(2)]
        barrier()
        sy.dma_start(b1c[:, :], ap["b1_c"][:, :]).then_inc(s_pre, 16)
        npre += 16
        Ws(se, "s", s_pre, "s_pre", npre)

        for c in range(2 * NCH):
            pe_begin(R["P8"])
            pe_end(te.matmul(P8[0:8, 0:1], msk_sl[:, c * E:(c + 1) * E],
                             onescol_bf[:, :], start=True, stop=True), R["P8"])
            rd_begin(ve, "v", R["P8"])
            tok = cv.inc(ve.tensor_copy(tot_sb[:, c:c + 1], P8[0:8, 0:1]))
            rd_end(cv, tok, R["P8"])
        W(te, "t", cv, cv.v)
        pe_begin(R["P8"])
        pe_end(te.transpose(P8[0:32, 8:16], tot_sb[:, :], ident32[0:8, 0:8]), R["P8"])
        rd_begin(ve, "v", R["P8"])
        tok = cv.inc(ve.tensor_copy(asb[0:32, 0:8], P8[0:32, 8:16]))
        rd_end(cv, tok, R["P8"])
        W(te, "t", cv, tok)
        pe_begin(R["P8"])
        pe_end(te.matmul(P8[0:32, 16:24], u1x32[:, :], asb[0:32, 0:8],
                         start=True, stop=True), R["P8"])
        rd_begin(ve, "v", R["P8"])
        tok = cv.inc(ve.tensor_copy(cb_sb[:, :], P8[0:32, 16:24]))
        rd_end(cv, tok, R["P8"])
        pos_tok = 0
        for c in range(2 * NCH):
            W(te, "t", cv, cv.v)
            pe_begin(R["P8"])
            te.matmul(P8[:, 24:32], u1bf[:, :], msk_sl[:, c * E:(c + 1) * E],
                      start=True, stop=True)
            pe_end(te.matmul(P8[0:1, 32:40], ident32[0:32, c:c + 1], cb_sb[:, :],
                             start=True, stop=True), R["P8"])
            rd_begin(ve, "v", R["P8"])
            tok = cv.inc(ve.tensor_copy(asb[:, :], P8[:, 24:32]))
            tok = cv.inc(ve.tensor_copy(rowsb[:, :], P8[0:1, 32:40]))
            rd_end(cv, tok, R["P8"])
            W(ve, "v", cv, tok)
            tok_row = cv.inc(ve.tensor_add(rowsb[:, :], rowsb[:, :], gbase_r[:, :]))
            W(te, "t", cv, tok_row)
            pe_begin(R["P8"])
            pe_end(te.matmul(P8[:, 40:48], ones_r[:, :], rowsb[:, :],
                             start=True, stop=True), R["P8"])
            W(ve, "v", cv, tok_row)
            tok = cv.inc(ve.tensor_tensor(t3sb[:, :], msk_sl[:, c * E:(c + 1) * E],
                                          asb[:, :], OP.mult))
            W(ve, "v", cv, tok)
            tok = cv.inc(ve.reduce_sum(rk_sb[:, :], t3sb[:, :], axis=AX.X))
            W(ve, "v", cv, tok)
            tok = cv.inc(ve.tensor_scalar(ov_sb[:, :], rk_sb[:, :], float(CAP), 1e9,
                                          OP.is_gt, OP.mult))
            rd_begin(ve, "v", R["P8"])
            tok = cv.inc(ve.tensor_add(absb[:, :], asb[:, :], P8[:, 40:48]))
            rd_end(cv, tok, R["P8"])
            W(ve, "v", cv, tok)
            tok = cv.inc(ve.tensor_tensor(t3sb[:, :], msk_sl[:, c * E:(c + 1) * E],
                                          absb[:, :], OP.mult))
            W(ve, "v", cv, tok)
            tok = cv.inc(ve.reduce_sum(posf[:, :], t3sb[:, :], axis=AX.X))
            W(ve, "v", cv, tok)
            tok = cv.inc(ve.scalar_tensor_tensor(posf[:, :], posf[:, :], -1.0,
                                                 ov_sb[:, :], OP.add, OP.add))
            W(ve, "v", cv, tok)
            pos_tok = cv.inc(ve.tensor_copy(pos_sl[:, c:c + 1], posf[:, :]))
        tok_z = cv.inc(ve.memset(ffn_acc[:, :], 0.0))
        W(sy, "sy", cv, tok_z)
        for e8 in range(E):
            sy.dma_start(
                xn_perm[e8 * CAP:(e8 + 1) * CAP, :].rearrange("(t p) c -> p t c", p=128),
                ffn_acc[:, :].rearrange("p (t c) -> p t c", c=D)).then_inc(s_t[e8 % 2], 16)
            t_cnt[e8 % 2] += 1
        for p_ in range(2):
            Ws(gp, "gp", s_t[p_], f"s_t{p_}", 16 * t_cnt[p_])
        W(gp, "gp", cv, pos_tok)
        for c in range(2 * NCH):
            par = c % 2
            Ws(sy, "sy", s_g[par], f"s_g{par}", 16 * g_cnt[par])
            sy.dma_start(xperm_t[par][:, :],
                         xn_tok[c * 128:(c + 1) * 128, :]).then_inc(s_r[par], 16)
            r_cnt[par] += 1
            Ws(gp, "gp", s_r[par], f"s_r{par}", 16 * r_cnt[par])
            gp.indirect_dma_start(
                xn_perm[:, :], bass.IndirectOffsetOnAxis(pos_sl[:, c:c + 1], 0),
                xperm_t[par][:, :], None,
                bounds_check=bc_reg, oob_is_err=False).then_inc(s_g[par], 16)
            g_cnt[par] += 1

        def w_load(it):
            e_, we_ = divmod(it, NWE)
            par = it % 3
            if w_rel[par] is not None:
                W(gp, "gp", w_rel[par][0], w_rel[par][1])
            gp.dma_start(
                w1e[par][:, :].rearrange("p (i h) -> p i h", h=512),
                ap["w1"][e_, :, we_ * 512:(we_ + 1) * 512].rearrange(
                    "(i p) h -> p i h", p=128)).then_inc(s_w[par], 16)
            gp.dma_start(
                w2e[par][:, :].rearrange("p (i h) -> p i h", h=1024),
                ap["w2"][e_, we_ * 512:(we_ + 1) * 512, :].rearrange(
                    "(i p) h -> p i h", p=128)).then_inc(s_w[par], 16)
            w_cnt[par] += 2

        tr_units = []
        xp_rel = [0, 0, 0, 0]
        xnTpe_rel = [0, 0]
        load_cnt = {}

        def emit_tile_load(e_, st):
            par = (e_ * NST + st) % 4
            pv = xp_rel[par]
            if pv:
                W(sy, "sy", ct, pv)
            Ws(sy, "sy", s_g[0], "s_g0", 16 * g_cnt[0])
            Ws(sy, "sy", s_g[1], "s_g1", 16 * g_cnt[1])
            sy.dma_start(xperm_t[par][:, :],
                         xn_perm[e_ * CAP + st * 128: e_ * CAP + (st + 1) * 128, :]
                         ).then_inc(s_r[par], 16)
            r_cnt[par] += 1
            load_cnt[(e_, st)] = r_cnt[par]

        def make_tr_unit(e_, st, halfu):
            par = (e_ * NST + st) % 4
            pe = e_ % 2

            def emit():
                Ws(te, "t", s_r[par], f"s_r{par}", 16 * load_cnt[(e_, st)])
                pe_begin(R["P3a"])
                last = None
                for j in range(4):
                    ic = halfu * 4 + j
                    last = te.matmul(P3a[:, j * 128:(j + 1) * 128],
                                     xperm_t[par][:, ic * 128:(ic + 1) * 128],
                                     identbf[:, :], start=True, stop=True)
                tok_t = pe_end(last, R["P3a"])
                if halfu == 1:
                    xp_rel[par] = tok_t
                rd_begin(ve, "v", R["P3a"])
                W(ve, "v", ct, xnTpe_rel[pe])
                dst = xnT_pe[pe][:, :].rearrange("p (i t) -> p i t", t=CAP)
                tok = cv.inc(ve.tensor_copy(
                    dst[:, halfu * 4:(halfu + 1) * 4, st * 128:(st + 1) * 128],
                    P3a[:, :].rearrange("p (j t) -> p j t", t=128)))
                rd_end(cv, tok, R["P3a"])
                if halfu == 1 and st + 4 < NST:
                    emit_tile_load(e_, st + 4)
            return emit

        def queue_expert_transposes(e_):
            for st0 in range(4):
                emit_tile_load(e_, st0)
            for st in range(NST):
                for halfu in range(2):
                    tr_units.append(make_tr_unit(e_, st, halfu))

        def pump_tr(n=1):
            for _ in range(n):
                if tr_units:
                    tr_units.pop(0)()

        barrier(("gp",))
        gp.dma_start(b2all[:, :], ap["b2r"][:, :]).then_inc(s_g[0], 16)
        g_cnt[0] += 1
        w_load(0)
        w_load(1)
        queue_expert_transposes(0)
        pump_tr(len(tr_units))
        h_rel = [None, None]
        ffn_w_tok = [0, 0]
        for it in range(E * NWE):
            e_, we_ = divmod(it, NWE)
            par = it % 3
            pe = e_ % 2
            if it + 2 < E * NWE:
                w_load(it + 2)
            if we_ == 0:
                if e_ + 1 < E:
                    queue_expert_transposes(e_ + 1)
            Ws(te, "t", s_w[par], f"s_w{par}", 16 * w_cnt[par])
            Ws(te, "t", s_g[0], "s_g0", 16 * g_cnt[0])
            for tb in range(NTB3):
                hp = tb % 2
                gelu_last = None
                for hc in range(HCE):
                    hreg = R["P0a"] if hc % 2 == 0 else R["P0b"]
                    cshift = 0 if hc % 2 == 0 else 512
                    if h_rel[hp] is not None:
                        W(se, "s", h_rel[hp][0], h_rel[hp][1])
                    pe_begin(hreg)
                    last = None
                    for ic in range(IC):
                        last = te.matmul(
                            P0[:, cshift:cshift + 512],
                            w1e[par][:, ic * 512 + hc * 128: ic * 512 + (hc + 1) * 128],
                            xnT_pe[pe][:, ic * CAP + tb * 512: ic * CAP + (tb + 1) * 512],
                            start=(ic == 0), stop=(ic == IC - 1))
                    pe_end(last, hreg)
                    rd_begin(se, "s", hreg)
                    gelu_last = cs.inc(se.activation(
                        h_sb[hp][:, hc * 512:(hc + 1) * 512], P0[:, cshift:cshift + 512],
                        AF.Gelu,
                        bias=b1c[:, e_ * 32 + we_ * 4 + hc: e_ * 32 + we_ * 4 + hc + 1]))
                    rd_end(cs, gelu_last, hreg)
                W(te, "t", cs, gelu_last)
                for tc2 in range(4):
                    tcg = tb * 4 + tc2
                    ra, rb = (R["P1a"], R["P1b"]) if tcg % 2 == 0 else (R["P2a"], R["P2b"])
                    PT = P1 if tcg % 2 == 0 else P2
                    pe_begin(ra, rb)
                    last = None
                    for hc in range(HCE):
                        for cb in range(2):
                            last = te.matmul(
                                PT[:, cb * 512:(cb + 1) * 512],
                                h_sb[hp][:, hc * 512 + tc2 * 128: hc * 512 + (tc2 + 1) * 128],
                                w2e[par][:, hc * 1024 + cb * 512: hc * 1024 + (cb + 1) * 512],
                                start=(hc == 0),
                                stop=(hc == HCE - 1 and we_ != 0))
                    if we_ == 0:
                        for cb in range(2):
                            last = te.matmul(PT[:, cb * 512:(cb + 1) * 512], ones_bf[0:1, 0:128],
                                             b2all[0:1, e_ * D + cb * 512: e_ * D + (cb + 1) * 512],
                                             start=False, stop=True)
                    pe_end(last, ra, rb)
                    rd_begin(ve, "v", ra, rb)
                    if we_ == 0:
                        Ws(ve, "v", s_t[0], "s_t0", 16 * t_cnt[0])
                        Ws(ve, "v", s_t[1], "s_t1", 16 * t_cnt[1])
                        Ws(ve, "v", s_fp, "s_fp", 16 * fp_cnt[0])
                        tok = cv.inc(ve.tensor_copy(
                            ffn_acc[:, tcg * D:(tcg + 1) * D], PT[:, :]))
                    else:
                        W(ve, "v", cv, cv.v)
                        tok = cv.inc(ve.tensor_tensor(
                            ffn_acc[:, tcg * D:(tcg + 1) * D],
                            ffn_acc[:, tcg * D:(tcg + 1) * D], PT[:, :], OP.add))
                    rd_end(cv, tok, ra, rb)
                    if we_ == NWE - 1:
                        ffn_w_tok[pe] = tok
                    pump_tr(1)
                h_rel[hp] = (ct, ct.v)
            w_rel[par] = (ct, ct.v)
            if we_ == NWE - 1:
                xnTpe_rel[pe] = ct.v
            if we_ == NWE - 1:
                W(gp, "gp", cv, ffn_w_tok[pe])
                gp.dma_start(
                    ffn_perm[e_ * CAP:(e_ + 1) * CAP, :].rearrange("(t p) c -> p t c", p=128),
                    ffn_acc[:, :].rearrange("p (t c) -> p t c", c=D)
                    ).then_inc(s_fp, 16)
                fp_cnt[0] += 1

        for p_ in range(2):
            Ws(gp, "gp", s_t[p_], f"s_t{p_}", 16 * t_cnt[p_])
            Ws(sy, "sy", s_t[p_], f"s_t{p_}", 16 * t_cnt[p_])
        Ws(gp, "gp", s_fp, "s_fp", 16 * fp_cnt[0])
        Ws(sy, "sy", s_fp, "s_fp", 16 * fp_cnt[0])
        x_rel = [None, None]
        for c in range(2 * NCH):
            par = c % 2
            if x_rel[par] is not None:
                W(sy, "sy", x_rel[par][0], x_rel[par][1])
            sy.dma_start(x_sb[par][:, :], ap["x"][c * 128:(c + 1) * 128, :]
                         ).then_inc(s_x[par], 16)
            x_cnt[par] += 1
            Ws(sy, "sy", s_o[par], f"s_o{par}", 16 * o_cnt[par])
            sy.dma_start(ybuf[par][:, :], y_tok[c * 128:(c + 1) * 128, :]
                         ).then_inc(s_r[par], 16)
            r_cnt[par] += 1
            Ws(gp, "gp", s_o[par], f"s_o{par}", 16 * o_cnt[par])
            gp.indirect_dma_start(
                fbuf[par][:, :], None,
                ffn_perm[:, :], bass.IndirectOffsetOnAxis(pos_sl[:, c:c + 1], 0),
                bounds_check=bc_reg, oob_is_err=False).then_inc(s_g[par], 16)
            g_cnt[par] += 1
            Ws(ve, "v", s_x[par], f"s_x{par}", 16 * x_cnt[par])
            Ws(ve, "v", s_r[par], f"s_r{par}", 16 * r_cnt[par])
            Ws(ve, "v", s_g[par], f"s_g{par}", 16 * g_cnt[par])
            if o_rel[par] is not None:
                W(ve, "v", o_rel[par][0], o_rel[par][1])
            Ws(ve, "v", s_o[par], f"s_o{par}", 16 * o_cnt[par])
            W(ve, "v", cv, cv.v)
            tok = cv.inc(ve.tensor_add(out_sb[par][:, :], x_sb[par][:, :], ybuf[par][:, :]))
            W(ve, "v", cv, tok)
            tok_o = cv.inc(ve.scalar_tensor_tensor(
                out_sb[par][:, :], fbuf[par][:, :], score_sl[:, c:c + 1],
                out_sb[par][:, :], OP.mult, OP.add))
            x_rel[par] = (cv, tok_o)
            o_rel[par] = (cv, tok_o)
            W(sy, "sy", cv, tok_o)
            sy.dma_start(out_d[c * 128:(c + 1) * 128, :], out_sb[par][:, :]
                         ).then_inc(s_o[par], 16)
            o_cnt[par] += 1
        ov_b.close()

        Ws(te, "t", s_o[0], "s_o0", 16 * o_cnt[0])
        Ws(te, "t", s_o[1], "s_o1", 16 * o_cnt[1])
    return nc


from concourse.bass_utils import run_bass_kernel_spmd

_NCORES = 8


def kernel(**inputs):
    nc = build(interleave=False)
    in_maps = [host_inputs(inputs, c) for c in range(_NCORES)]
    res = run_bass_kernel_spmd(nc, in_maps, core_ids=list(range(_NCORES)))
    return np.stack([np.asarray(res.results[c]["out"]) for c in range(_NCORES)], axis=0)


# version 22
# speedup vs baseline: 1.0583x; 1.0583x over previous
import numpy as np
import ml_dtypes
from contextlib import ExitStack
import concourse.bass as bass
import concourse.mybir as mybir

F32 = mybir.dt.float32
BF16 = mybir.dt.bfloat16
U32 = mybir.dt.uint32
U8 = mybir.dt.uint8
AF = mybir.ActivationFunctionType
OP = mybir.AluOpType
AX = mybir.AxisListType

T, D, HALF = 4096, 1024, 2048
NCH = 16
IC = 8
E, HID = 8, 4096
NWE, HCE, NTB = 8, 4, 4
NG, GW, HD = 2, 512, 64
CAP = 1536
NSLOT = E * CAP
NST = CAP // 128
NTB3 = CAP // 512
SCALE = HD ** -0.5


def host_inputs(full, core):
    f32 = np.float32
    bf = ml_dtypes.bfloat16
    return {
        "x": np.ascontiguousarray(full["L0_attn_x"][core]),
        "wqT": np.ascontiguousarray(full["L0_attn_Wq"].T),
        "wkT": np.ascontiguousarray(full["L0_attn_Wk"].T),
        "wvT": np.ascontiguousarray(full["L0_attn_Wv"].T),
        "wbT": np.ascontiguousarray(full["L0_attn_Wbeta"].T),
        "waT": np.ascontiguousarray(full["L0_attn_Walpha"].T),
        "bq_c": np.ascontiguousarray(full["L0_attn_bq"].reshape(IC, 128).T),
        "bk_c": np.ascontiguousarray(full["L0_attn_bk"].reshape(IC, 128).T),
        "bk_r": np.ascontiguousarray(full["L0_attn_bk"].reshape(1, D)),
        "bq_r": np.ascontiguousarray(full["L0_attn_bq"].reshape(1, D)),
        "bv_r": np.ascontiguousarray(full["L0_attn_bv"].reshape(1, D)),
        "bb_r": np.ascontiguousarray(full["L0_attn_bbeta"].reshape(1, D)),
        "ba_r": np.ascontiguousarray(full["L0_attn_balpha"].reshape(1, D)),
        "wgred": full["L0_ffn_wg_red"],
        "wg": full["L0_ffn_wg"],
        "w1": full["L0_ffn_w1"],
        "b1_c": np.ascontiguousarray(
            full["L0_ffn_b1"].reshape(E, 32, 128).transpose(2, 0, 1).reshape(128, 256)),
        "w2": full["L0_ffn_w2"],
        "b2": full["L0_ffn_b2"],
        "ident32": np.eye(128, dtype=f32),
        "u64": np.ascontiguousarray((np.triu(np.ones((128, 128), f32)) / 64.0).astype(f32)),
        "u1bf": np.ascontiguousarray(np.triu(np.ones((128, 128), bf))),
        "u1x32": np.ascontiguousarray(np.triu(np.ones((32, 32), f32), k=1)),
        "onescol_bf": np.ones((128, 1), bf),
        "gbase": (np.arange(8, dtype=f32) * 1536.0).reshape(1, 8),
        "b2r": np.ascontiguousarray(full["L0_ffn_b2"].reshape(1, 8 * 1024)),
        "identbf": np.eye(128, dtype=bf),
        "triu8": np.ascontiguousarray(np.tile(np.triu(np.ones((128, 128), np.uint8)), (1, 8))),
        "onehot8": np.ascontiguousarray(
            np.concatenate([np.eye(8, dtype=f32)[:, h:h + 1] * np.ones((1, 128), f32)
                            for h in range(8)], axis=1)),
        "posbig": np.full((128, 1024), 1e9, f32),
        "ones_r": np.ones((1, 128), f32),
        "ones_bf": np.ones((1, 512), bf),
    }


class Ctr:
    def __init__(self, es, nc, name):
        self.sem = es.enter_context(nc.semaphore(name))
        self.v = 0
        self.name = name

    def inc(self, ins, n=1):
        ins.then_inc(self.sem, n)
        self.v += n
        return self.v


class Reg:
    def __init__(self, name):
        self.name = name
        self.w = None
        self.r = {}


def build(interleave=True, dbg=False):
    nc = bass.Bass("TRN2", target_bir_lowering=False)
    ap = {}

    def din(name, shape, dt=F32):
        ap[name] = nc.dram_tensor(name, list(shape), dt, kind="ExternalInput").ap()
        return ap[name]

    din("x", (T, D))
    for n in ("wqT", "wkT", "wvT", "wbT", "waT"):
        din(n, (D, D))
    din("bq_c", (128, IC)); din("bk_c", (128, IC))
    din("bk_r", (1, D)); din("bq_r", (1, D))
    din("bv_r", (1, D)); din("bb_r", (1, D)); din("ba_r", (1, D))
    din("wgred", (16, D)); din("wg", (E, 16))
    din("w1", (E, D, HID)); din("b1_c", (128, 256))
    din("w2", (E, HID, D)); din("b2", (E, D))
    din("ident32", (128, 128))
    din("u64", (128, 128)); din("triu8", (128, 1024), U8)
    din("u1bf", (128, 128), BF16); din("u1x32", (32, 32)); din("onescol_bf", (128, 1), BF16)
    din("gbase", (1, 8)); din("b2r", (1, 8 * 1024)); din("identbf", (128, 128), BF16)
    din("onehot8", (8, 8 * 128)); din("posbig", (128, 1024))
    din("ones_r", (1, 128)); din("ones_bf", (1, 512), BF16)

    out_d = nc.dram_tensor("out", [T, D], F32, kind="ExternalOutput").ap()
    xn_tok = nc.dram_tensor("xn_tok", [T, D], BF16, kind="Internal").ap()
    xn_perm = nc.dram_tensor("xn_perm", [NSLOT, D], BF16, kind="Internal").ap()
    ffn_perm = nc.dram_tensor("ffn_perm", [NSLOT, D], BF16, kind="Internal").ap()
    y_tok = nc.dram_tensor("y_tok", [T, D], BF16, kind="Internal").ap()

    with ExitStack() as es:
        sb = lambda name, shape, dt=F32: es.enter_context(nc.sbuf_tensor(name, list(shape), dt))
        sem = lambda n: es.enter_context(nc.semaphore(n))
        ct, cv, cs = Ctr(es, nc, "ct"), Ctr(es, nc, "cv"), Ctr(es, nc, "cs")
        te, ve, se, sy, gp = nc.tensor, nc.vector, nc.scalar, nc.sync, nc.gpsimd
        waited = {}

        def W(eng, ename, ctr, tok):
            if tok is None or tok <= 0:
                return
            key = (ename, ctr.name)
            if waited.get(key, 0) >= tok:
                return
            eng.wait_ge(ctr.sem, tok)
            waited[key] = tok

        def Ws(eng, ename, semh, sname, tok):
            if tok <= 0:
                return
            key = (ename, sname)
            if waited.get(key, 0) >= tok:
                return
            eng.wait_ge(semh, tok)
            waited[key] = tok

        def pe_begin(*rs):
            for reg in rs:
                for (ctr_, tok_) in reg.r.values():
                    W(te, "t", ctr_, tok_)
                reg.r = {}

        def pe_end(last_ins, *rs):
            tok = ct.inc(last_ins)
            for reg in rs:
                reg.w = (ct, tok)
            return tok

        def rd_begin(eng, ename, *rs):
            for reg in rs:
                if reg.w is not None:
                    W(eng, ename, reg.w[0], reg.w[1])

        def rd_end(ctr, tok, *rs):
            for reg in rs:
                reg.r[ctr.name] = (ctr, tok)

        score_sl = sb("score_sl", (128, 2 * NCH))
        idx_sl = sb("idx_sl", (128, 2 * NCH))
        pos_sl = sb("pos_sl", (128, 2 * NCH), U32)
        msk_sl = sb("msk_sl", (128, 2 * NCH * E), BF16)
        state = [sb(f"state{g}", (128, 512)) for g in range(NG)]
        state_bf = sb("state_bf", (128, 512), BF16)
        x_sb = [sb(f"x_sb{i}", (128, D)) for i in range(2)]
        out_sb = [sb(f"out_sb{i}", (128, D)) for i in range(2)]
        xn_sb = out_sb[0]
        xnT32 = out_sb[1]
        ident32 = sb("c_id32", (128, 128))
        identbf = sb("c_idbf", (128, 128), BF16)
        u64 = sb("c_u64", (128, 128))
        u1bf = sb("c_u1bf", (128, 128), BF16)
        u1x32 = sb("c_u1x32", (32, 32))
        onescol_bf = sb("c_ocbf", (128, 1), BF16)
        gbase_r = sb("c_gbase", (1, 8))
        triu8 = sb("c_triu", (128, 1024), U8)
        onehot8 = sb("c_oh8", (8, 8 * 128))
        posbig = sb("c_posbig", (128, 1024))
        ones_r = sb("c_ones", (1, 128))
        ones_bf = sb("c_onesbf", (1, 512), BF16)
        c_eps = sb("c_eps", (128, 1)); c_one = sb("c_one", (128, 1))
        bq_c = sb("p_bqc", (128, IC)); bk_c = sb("p_bkc", (128, IC))
        bqs = sb("p_bqs", (128, IC)); bks = sb("p_bks", (128, IC))
        brow4 = [sb(f"p_brow{i}", (1, GW), BF16) for i in range(4)]
        wred8 = sb("p_wred8", (128, IC * E))
        bnst = [sb(f"g_bnst{i}", (128, 12)) for i in range(2)]
        mv = [sb(f"g_mv{i}", (128, 2)) for i in range(2)]
        stdv = [sb(f"g_std{i}", (128, 1)) for i in range(2)]
        rstd = [sb(f"g_rstd{i}", (128, 1)) for i in range(2)]
        lg = [sb(f"g_lg{i}", (128, 8)) for i in range(2)]
        mx8 = [sb(f"g_mx8{i}", (128, 8)) for i in range(2)]
        ix8 = [sb(f"g_ix8{i}", (128, 8), U32) for i in range(2)]
        expd = [sb(f"g_expd{i}", (128, 8)) for i in range(2)]
        ngmx = [sb(f"g_ngmx{i}", (128, 1)) for i in range(2)]
        ssum = [sb(f"g_ssum{i}", (128, 1)) for i in range(2)]
        tot_sb = sb("pm_tot", (8, 32)); cb_sb = sb("pm_cb", (32, 8))
        rowsb = sb("pm_row", (1, 8)); asb = sb("pm_a", (128, 8)); absb = sb("pm_ab", (128, 8))
        t3sb = sb("pm_t3", (128, 8))
        rk_sb = sb("pm_rk", (128, 1)); ov_sb = sb("pm_ov", (128, 1)); posf = sb("pm_posf", (128, 1))

        ov_a = ExitStack()
        asb_ = lambda name, shape, dt=F32: ov_a.enter_context(nc.sbuf_tensor(name, list(shape), dt))
        xnT = asb_("xnT", (128, IC * HALF), BF16)
        slab = {n: asb_(f"at_{n}", (128, IC * GW), BF16)
                for n in ("wq", "wk", "wv", "wb", "wa")}
        q_sl = [asb_(f"q_sl{i}", (128, 2048), BF16) for i in range(2)]
        k_sl = [asb_(f"k_sl{i}", (128, 2048), BF16) for i in range(2)]
        v_sl = asb_("v_sl", (128, 512), BF16)
        vb_sl = [asb_(f"vb_sl{i}", (128, 2048), BF16) for i in range(2)]
        vbm_sl = [asb_(f"vbm_sl{i}", (128, 2048), BF16) for i in range(2)]
        ktok = [asb_(f"ktok{i}", (128, 2048), BF16) for i in range(2)]
        adT = [asb_(f"adT{i}", (128, 1024), BF16) for i in range(2)]
        browq = asb_("browq", (1, GW), BF16)
        browk = asb_("browk", (1, GW), BF16)
        L1 = asb_("L1", (128, GW)); CSb = asb_("CSb", (128, GW))
        NEG = asb_("NEG", (128, GW)); M2 = asb_("M2", (128, GW))
        BSP = asb_("BSP", (128, GW)); JUNK = asb_("JUNK", (128, GW))
        Dsb = asb_("Dsb", (128, 1024)); Dm_all = asb_("Dm_all", (128, 1024))
        dec_all = asb_("dec_all", (128, 1024))
        yi_sb = asb_("yi_sb", (128, GW)); t1_sb = asb_("t1_sb", (128, GW))
        tmc = asb_("tmc", (128, GW))
        lasum = [asb_(f"a_lasum{i}", (128, 32)) for i in range(2)]
        bmsum = asb_("a_bmsum", (128, 8))
        cums_b = [asb_(f"a_cums{i}", (128, 32)) for i in range(2)]
        cumsT_b = [asb_(f"a_cumsT{i}", (8, 512)) for i in range(2)]
        curve_b = [asb_(f"a_curve{i}", (128, 32)) for i in range(2)]
        asumb_b = [asb_(f"a_asumb{i}", (128, 32)) for i in range(2)]
        xnbf = [asb_(f"xnbf{i}", (128, D), BF16) for i in range(2)]
        ybf = [asb_(f"ybf{i}", (128, GW), BF16) for i in range(2)]
        st_ov = ExitStack()
        ssb = lambda name, shape, dt=F32: st_ov.enter_context(nc.sbuf_tensor(name, list(shape), dt))
        wg_sb = ssb("p_wg", (E, 16)); wgred_sb = ssb("p_wgred", (16, 512))
        wguT = ssb("p_wguT", (16, E)); wred8T = ssb("p_wred8T", (E, 512))
        wgsq = ssb("p_wgsq", (E, 16))
        nrm = ssb("p_nrm", (E, 1)); nrm2 = ssb("p_nrm2", (E, 1)); rinv = ssb("p_rinv", (E, 1))
        wgu = ssb("p_wgu", (E, 16))

        P0 = es.enter_context(nc.psum_tensor("P0", [128, 1024], F32))
        P1 = es.enter_context(nc.psum_tensor("P1", [128, 1024], F32))
        P2 = es.enter_context(nc.psum_tensor("P2", [128, 1024], F32))
        P3a = es.enter_context(nc.psum_tensor("P3a", [128, 512], F32))
        P8 = es.enter_context(nc.psum_tensor("P8", [128, 512], F32))
        R = {n: Reg(n) for n in ("P0a", "P0b", "P1a", "P1b", "P2a", "P2b", "P3a", "P8")}

        s_pre = sem("s_pre")
        s_x = [sem("s_x0"), sem("s_x1")]
        x_cnt = [0, 0]; x_rel = [None, None]
        s_w = [sem("s_w0"), sem("s_w1"), sem("s_w2")]
        s_fp = sem("s_fp")
        w_cnt = [0, 0, 0]; w_rel = [None, None, None]; fp_cnt = [0]
        s_wa = sem("s_wa")
        wa_cnt = [0]; wa_rel = [None]
        s_o = [sem("s_o0"), sem("s_o1")]
        o_cnt = [0, 0]; o_rel = [None, None]
        s_t = [sem("s_t0"), sem("s_t1")]
        t_cnt = [0, 0]
        s_r = [sem("s_r0"), sem("s_r1"), sem("s_r2"), sem("s_r3")]
        s_gs = [sem("s_gs0"), sem("s_gs1"), sem("s_gs2"), sem("s_gs3")]
        gs_cnt = [0, 0, 0, 0]
        r_cnt = [0, 0, 0, 0]
        s_g = [sem("s_g0"), sem("s_g1")]
        g_cnt = [0, 0]
        bc_reg = es.enter_context(gp.register("bc"))
        gp.reg_mov(bc_reg, NSLOT - 1)

        def barrier(engines=("t", "v", "s", "sy", "gp")):
            objs = {"t": te, "v": ve, "s": se, "sy": sy, "gp": gp}
            for en in engines:
                eng = objs[en]
                for ctr in (ct, cv, cs):
                    W(eng, en, ctr, ctr.v)
                for p_ in range(2):
                    Ws(eng, en, s_o[p_], f"s_o{p_}", 16 * o_cnt[p_])
                    Ws(eng, en, s_t[p_], f"s_t{p_}", 16 * t_cnt[p_])
                    Ws(eng, en, s_r[p_], f"s_r{p_}", 16 * r_cnt[p_])
                    Ws(eng, en, s_g[p_], f"s_g{p_}", 16 * g_cnt[p_])
                    Ws(eng, en, s_gs[p_], f"s_gs{p_}", 16 * gs_cnt[p_])
                    Ws(eng, en, s_gs[p_ + 2], f"s_gs{p_ + 2}", 16 * gs_cnt[p_ + 2])
                    Ws(eng, en, s_x[p_], f"s_x{p_}", 16 * x_cnt[p_])
                for p_ in range(3):
                    Ws(eng, en, s_w[p_], f"s_w{p_}", 16 * w_cnt[p_])
                Ws(eng, en, s_wa, "s_wa", 16 * wa_cnt[0])
                Ws(eng, en, s_fp, "s_fp", 16 * fp_cnt[0])

        def x_load(par, row0):
            if x_rel[par] is not None:
                W(sy, "sy", x_rel[par][0], x_rel[par][1])
            sy.dma_start(x_sb[par][:, :], ap["x"][row0:row0 + 128, :]).then_inc(s_x[par], 16)
            x_cnt[par] += 1

        def x_wait(eng, ename, par):
            Ws(eng, ename, s_x[par], f"s_x{par}", 16 * x_cnt[par])

        npre = 0
        for dst, srcn in ((ident32, "ident32"), (identbf, "identbf"), (u64, "u64"),
                          (u1bf, "u1bf"), (u1x32, "u1x32"), (onescol_bf, "onescol_bf"),
                          (gbase_r, "gbase"),
                          (triu8, "triu8"), (posbig, "posbig"), (onehot8, "onehot8"),
                          (ones_r, "ones_r"), (ones_bf, "ones_bf"),
                          (bq_c, "bq_c"), (bk_c, "bk_c"),
                          (wg_sb, "wg")):
            sy.dma_start(dst[:, :], ap[srcn][:, :]).then_inc(s_pre, 16)
            npre += 16



        Ws(se, "s", s_pre, "s_pre", npre)
        Ws(ve, "v", s_pre, "s_pre", npre)
        Ws(te, "t", s_pre, "s_pre", npre)
        tok = cs.inc(se.activation(wgsq[:, :], wg_sb[:, :], AF.Square, accum_out=nrm2[:, :]))
        W(se, "s", cs, tok)
        tok = cs.inc(se.activation(nrm[:, :], nrm2[:, :], AF.Sqrt))
        W(ve, "v", cs, tok)
        tok = cv.inc(ve.tensor_scalar_max(nrm[:, :], nrm[:, :], 1e-4))
        W(ve, "v", cv, tok)
        tok = cv.inc(ve.reciprocal(rinv[:, :], nrm[:, :]))
        W(ve, "v", cv, tok)
        tok_wgu = cv.inc(ve.tensor_scalar_mul(wgu[:, :], wg_sb[:, :], rinv[:, :]))
        tok_bqs = cv.inc(ve.tensor_scalar_mul(bqs[:, :], bq_c[:, :], SCALE))
        tok_bks = cv.inc(ve.tensor_scalar_mul(bks[:, :], bk_c[:, :], SCALE))
        for g in range(NG):
            cv.inc(ve.memset(state[g][:, :], 0.0))
        tok = cv.inc(ve.memset(c_eps[:, :], 1e-5))
        tok_c1 = cv.inc(ve.memset(c_one[:, :], 1.0))
        W(se, "s", cv, tok_c1)
        W(te, "t", cv, tok_wgu)
        pe_begin(R["P3a"])
        pe_end(te.transpose(P3a[0:16, 0:8], wgu[:, :], ident32[0:8, 0:8]), R["P3a"])
        rd_begin(ve, "v", R["P3a"])
        tok = cv.inc(ve.tensor_copy(wguT[:, :], P3a[0:16, 0:8]))
        rd_end(cv, tok, R["P3a"])
        for half_c in range(2):
            W(sy, "sy", ct, ct.v)
            sy.dma_start(wgred_sb[:, :],
                         ap["wgred"][:, half_c * 512:(half_c + 1) * 512]).then_inc(s_pre, 16)
            npre += 16
            Ws(te, "t", s_pre, "s_pre", npre)
            W(te, "t", cv, tok)
            pe_begin(R["P3a"])
            pe_end(te.matmul(P3a[0:8, 0:512], wguT[:, :], wgred_sb[:, :],
                             start=True, stop=True), R["P3a"])
            rd_begin(ve, "v", R["P3a"])
            tok = cv.inc(ve.tensor_copy(wred8T[:, :], P3a[0:8, 0:512]))
            rd_end(cv, tok, R["P3a"])
            for ic4 in range(4):
                ic = half_c * 4 + ic4
                W(te, "t", cv, tok)
                pe_begin(R["P3a"])
                pe_end(te.transpose(P3a[:, 0:8], wred8T[:, ic4 * 128:(ic4 + 1) * 128],
                                    ident32[0:8, 0:8]), R["P3a"])
                rd_begin(ve, "v", R["P3a"])
                tok = cv.inc(ve.tensor_copy(wred8[:, ic * 8:(ic + 1) * 8], P3a[:, 0:8]))
                rd_end(cv, tok, R["P3a"])
        st_ov.close()

        SC_T = 512
        NSC = HALF // SC_T

        def attn_phase(half):
            hb = half * HALF
            slab_rel = [0, 0]
            adT_rel = [0, 0]
            mem_tok = [0]
            exp_rel = [0]
            admul_rel = [0]
            curve_tok = [0, 0]
            ct2_tok = [0, 0]
            proj_cs = [0, 0]; proj_cv = [0, 0]
            sbf_tok = [0]
            for grp in range(NG):
                gofs = grp * GW
                if wa_rel[0] is not None:
                    W(gp, "gp", wa_rel[0][0], wa_rel[0][1])
                for n, wn in (("wq", "wqT"), ("wk", "wkT"), ("wv", "wvT"),
                              ("wb", "wbT"), ("wa", "waT")):
                    gp.dma_start(
                        slab[n][:, :].rearrange("p (i o) -> p i o", o=GW),
                        ap[wn][:, gofs:gofs + GW].rearrange("(i p) o -> p i o", p=128)
                    ).then_inc(s_wa, 16)
                    wa_cnt[0] += 1
                for i, bn in enumerate(("bv_r", "bb_r", "ba_r", "bk_r")):
                    gp.dma_start(brow4[i][:, :], ap[bn][0:1, gofs:gofs + GW]).then_inc(s_wa, 16)
                    wa_cnt[0] += 1
                gp.dma_start(browq[:, :], ap["bq_r"][0:1, gofs:gofs + GW]).then_inc(s_wa, 16)
                gp.dma_start(browk[:, :], ap["bk_r"][0:1, gofs:gofs + GW]).then_inc(s_wa, 16)
                wa_cnt[0] += 2
                Ws(te, "t", s_wa, "s_wa", 16 * wa_cnt[0])

                pseq = [0]
                W(ve, "v", ct, mem_tok[0])
                W(ve, "v", cv, cv.v)
                sbf_tok[0] = cv.inc(ve.tensor_copy(state_bf[:, :], state[grp][:, :]))

                def proj_gen(sc):
                    p = sc % 2
                    W(se, "s", ct, slab_rel[p])
                    W(ve, "v", ct, slab_rel[p])
                    for wname, dstq, brq in (("wq", q_sl[p], browq), ("wk", k_sl[p], browk)):
                        for oc in range(4):
                            reg = R["P0a"] if pseq[0] % 2 == 0 else R["P0b"]
                            cof = 0 if pseq[0] % 2 == 0 else 512
                            pseq[0] += 1
                            pe_begin(reg)
                            for ic in range(IC):
                                te.matmul(
                                    P0[:, cof:cof + 512],
                                    slab[wname][:, ic * GW + oc * 128: ic * GW + (oc + 1) * 128],
                                    xnT[:, ic * HALF + sc * SC_T: ic * HALF + (sc + 1) * SC_T],
                                    start=(ic == 0), stop=False)
                            last = te.matmul(P0[:, cof:cof + 512],
                                             brq[0:1, oc * 128:(oc + 1) * 128],
                                             ones_bf[0:1, 0:512], start=False, stop=True)
                            pe_end(last, reg)
                            rd_begin(se, "s", reg)
                            W(se, "s", cv, cv.v)
                            tok = cs.inc(se.mul(dstq[:, oc * 512:(oc + 1) * 512],
                                                P0[:, cof:cof + 512], SCALE))
                            rd_end(cs, tok, reg)
                            yield
                    for tc2 in range(4):
                        for wi, wname in enumerate(("wv", "wb", "wa", "wk")):
                            reg = R["P0a"] if pseq[0] % 2 == 0 else R["P0b"]
                            cof = 0 if pseq[0] % 2 == 0 else 512
                            pseq[0] += 1
                            pe_begin(reg)
                            for ic in range(IC):
                                te.matmul(
                                    P0[:, cof:cof + 512],
                                    xnT[:, ic * HALF + sc * SC_T + tc2 * 128:
                                        ic * HALF + sc * SC_T + (tc2 + 1) * 128],
                                    slab[wname][:, ic * GW:(ic + 1) * GW],
                                    start=(ic == 0), stop=False)
                            brow_i = brow4[3] if wi == 3 else brow4[wi]
                            last = te.matmul(P0[:, cof:cof + 512], ones_bf[0:1, 0:128],
                                             brow_i[:, :], start=False, stop=True)
                            pe_end(last, reg)
                            if wi == 0:
                                rd_begin(se, "s", reg)
                                W(se, "s", cv, cv.v)
                                tok_v = cs.inc(se.copy(v_sl[:, :],
                                                       P0[:, cof:cof + 512]))
                                rd_end(cs, tok_v, reg)
                            elif wi == 3:
                                rd_begin(se, "s", reg)
                                W(se, "s", cv, cv.v)
                                tok = cs.inc(se.mul(ktok[p][:, tc2 * GW:(tc2 + 1) * GW],
                                                    P0[:, cof:cof + 512], SCALE))
                                rd_end(cs, tok, reg)
                            elif wi == 1:
                                rd_begin(se, "s", reg)
                                W(se, "s", cv, cv.v)
                                tok_l1 = cs.inc(se.copy(L1[:, :], P0[:, cof:cof + 512]))
                                rd_end(cs, tok_l1, reg)
                                W(ve, "v", cs, tok_l1)
                                tok = cv.inc(ve.scalar_tensor_tensor(
                                    M2[:, :], L1[:, :], -1.0, L1[:, :], OP.mult, OP.min))
                                W(se, "s", cv, tok)
                                tok = cs.inc(se.activation(M2[:, :], M2[:, :], AF.Exp))
                                W(se, "s", cs, tok)
                                tok_ln = cs.inc(se.activation(M2[:, :], M2[:, :], AF.Ln,
                                                              bias=c_one[:, :]))
                                W(ve, "v", cs, tok_ln)
                                for h in range(8):
                                    tok_sp = cv.inc(ve.scalar_tensor_tensor(
                                        BSP[:, h * 64:(h + 1) * 64], L1[:, h * 64:(h + 1) * 64],
                                        0.0, M2[:, h * 64:(h + 1) * 64], OP.max, OP.add,
                                        accum_out=bmsum[:, h:h + 1]))
                                W(ve, "v", cv, tok_sp)
                                tok_vb = cv.inc(ve.tensor_mul(
                                    vb_sl[p][:, tc2 * GW:(tc2 + 1) * GW],
                                    v_sl[:, :], BSP[:, :]))
                                vview = v_sl[:, :].rearrange(
                                    "p (h d) -> p h d", d=64)
                                mview = vbm_sl[p][:, tc2 * GW:(tc2 + 1) * GW].rearrange(
                                    "p (h d) -> p h d", d=64)
                                bmv = bmsum[:, :].rearrange("p (h d) -> p h d", d=1)
                                b0, b1 = bass.broadcast_tensor_aps(vview, bmv)
                                tok = cv.inc(ve.scalar_tensor_tensor(
                                    mview, b0, 1.0 / 64, b1, OP.mult, OP.mult))
                            else:
                                rd_begin(se, "s", reg)
                                W(se, "s", cv, cv.v)
                                tok_l1 = cs.inc(se.copy(L1[:, :], P0[:, cof:cof + 512]))
                                rd_end(cs, tok_l1, reg)
                                W(ve, "v", cs, tok_l1)
                                tok = cv.inc(ve.tensor_scalar(CSb[:, :], L1[:, :], 10.0, -10.0,
                                                              OP.min, OP.max))
                                W(ve, "v", cv, tok)
                                tok = cv.inc(ve.tensor_scalar_mul(NEG[:, :], CSb[:, :], -1.0))
                                W(ve, "v", cv, tok)
                                tok = cv.inc(ve.scalar_tensor_tensor(
                                    M2[:, :], CSb[:, :], -1.0, CSb[:, :], OP.mult, OP.min))
                                W(se, "s", cv, tok)
                                tok = cs.inc(se.activation(M2[:, :], M2[:, :], AF.Exp))
                                W(se, "s", cs, tok)
                                tok_ln = cs.inc(se.activation(M2[:, :], M2[:, :], AF.Ln,
                                                              bias=c_one[:, :]))
                                W(ve, "v", cs, tok_ln)
                                for h in range(8):
                                    cv.inc(ve.scalar_tensor_tensor(
                                        JUNK[:, h * 64:(h + 1) * 64], NEG[:, h * 64:(h + 1) * 64],
                                        0.0, M2[:, h * 64:(h + 1) * 64], OP.max, OP.add,
                                        accum_out=lasum[p][:, tc2 * 8 + h: tc2 * 8 + h + 1]))
                            yield
                    proj_cs[p] = cs.v
                    proj_cv[p] = cv.v

                def smalls(sc):
                    p = sc % 2
                    W(te, "t", cv, proj_cv[p])
                    W(te, "t", cs, proj_cs[p])
                    pe_begin(R["P8"])
                    last = None
                    for c2 in range(4):
                        last = te.matmul(P8[:, 384 + c2 * 8:392 + c2 * 8], u64[:, :],
                                         lasum[p][:, c2 * 8:(c2 + 1) * 8],
                                         start=True, stop=True)
                    pe_end(last, R["P8"])
                    pe_begin(R["P1a"], R["P1b"])
                    last = None
                    for c2 in range(4):
                        last = te.matmul(P1[0:8, c2 * 128:(c2 + 1) * 128],
                                         lasum[p][:, c2 * 8:(c2 + 1) * 8],
                                         u64[:, :], start=True, stop=True)
                    pe_end(last, R["P1a"], R["P1b"])
                    rd_begin(ve, "v", R["P8"], R["P1a"], R["P1b"])
                    tok = cv.inc(ve.tensor_copy(cums_b[p][:, :], P8[:, 384:416]))
                    tok_ct2 = cv.inc(ve.tensor_copy(cumsT_b[p][:, :], P1[0:8, 0:512]))
                    rd_end(cv, tok_ct2, R["P8"], R["P1a"], R["P1b"])
                    ct2_tok[p] = tok_ct2
                    W(se, "s", cv, tok_ct2)
                    W(se, "s", cv, cv.v)
                    curve_tok[p] = cs.inc(se.activation(
                        curve_b[p][:, :], cums_b[p][:, :], AF.Exp, scale=-1.0))
                    W(te, "t", cv, tok_ct2)
                    pe_begin(R["P8"])
                    last = None
                    for c2 in range(4):
                        last = te.matmul(P8[0:1, 416 + c2 * 8:424 + c2 * 8],
                                         cumsT_b[p][:, c2 * 128 + 127: c2 * 128 + 128],
                                         ident32[0:8, 0:8], start=True, stop=True)
                    pe_end(last, R["P8"])
                    rd_begin(se, "s", R["P8"])
                    tok_as = cs.inc(se.activation(JUNK[0:1, 0:32], P8[0:1, 416:448],
                                                  AF.Exp, scale=-1.0))
                    rd_end(cs, tok_as, R["P8"])
                    W(te, "t", cs, tok_as)
                    pe_begin(R["P8"])
                    last = None
                    for c2 in range(4):
                        last = te.matmul(P8[:, 448 + c2 * 8:456 + c2 * 8], ones_r[:, :],
                                         JUNK[0:1, c2 * 8:(c2 + 1) * 8],
                                         start=True, stop=True)
                    pe_end(last, R["P8"])
                    rd_begin(ve, "v", R["P8"])
                    tok = cv.inc(ve.tensor_copy(asumb_b[p][:, :], P8[:, 448:480]))
                    rd_end(cv, tok, R["P8"])

                def prep(k):
                    sc, c2 = divmod(k, 4)
                    p = sc % 2
                    W(te, "t", cv, ct2_tok[p])
                    W(te, "t", cs, proj_cs[p])
                    W(te, "t", cv, proj_cv[p])
                    pe_begin(R["P1a"], R["P1b"])
                    last = None
                    for h in range(8):
                        last = te.matmul(P1[:, h * 128:(h + 1) * 128],
                                         onehot8[:, h * 128:(h + 1) * 128],
                                         cumsT_b[p][:, c2 * 128:(c2 + 1) * 128],
                                         start=True, stop=True)
                    pe_end(last, R["P1a"], R["P1b"])
                    pe_begin(R["P2a"], R["P2b"])
                    last = None
                    for h in range(8):
                        oc, poff = h // 2, (h % 2) * 64
                        dst = P2[:, (h // 2) * 128:(h // 2 + 1) * 128] if h % 2 == 0 \
                            else P2[:, 512 + (h // 2) * 128: 512 + (h // 2 + 1) * 128]
                        last = te.matmul(
                            dst,
                            k_sl[p][poff:poff + 64, oc * 512 + c2 * 128: oc * 512 + (c2 + 1) * 128],
                            q_sl[p][poff:poff + 64, oc * 512 + c2 * 128: oc * 512 + (c2 + 1) * 128],
                            start=True, stop=True)
                    pe_end(last, R["P2a"], R["P2b"])
                    rd_begin(ve, "v", R["P1a"], R["P1b"])
                    bview = P1[:, :].rearrange("p (h i) -> p h i", i=128)
                    cview = cums_b[p][:, c2 * 8:(c2 + 1) * 8].rearrange("p (h i) -> p h i", i=1)
                    b0, b1 = bass.broadcast_tensor_aps(bview, cview)
                    tok = cv.inc(ve.tensor_tensor(
                        Dsb[:, :].rearrange("p (h i) -> p h i", i=128), b0, b1, OP.subtract))
                    rd_end(cv, tok, R["P1a"], R["P1b"])
                    W(ve, "v", cv, tok)
                    W(ve, "v", cs, exp_rel[0])
                    tok_dm = cv.inc(ve.select(Dm_all[:, :], triu8[:, :], Dsb[:, :],
                                              posbig[:, :], add_drain=True))
                    W(se, "s", cv, tok_dm)
                    W(se, "s", cv, admul_rel[0])
                    tok_dc = cs.inc(se.activation(dec_all[:, :], Dm_all[:, :], AF.Exp,
                                                  scale=-1.0))
                    exp_rel[0] = tok_dc
                    W(ve, "v", cs, tok_dc)
                    W(ve, "v", ct, adT_rel[k % 2])
                    rd_begin(ve, "v", R["P2a"], R["P2b"])
                    tok_ad = None
                    for par2 in range(2):
                        srcp = P2[:, 0:512] if par2 == 0 else P2[:, 512:1024]
                        aview = srcp.rearrange("p (a o i) -> p a o i", o=1, i=128)
                        dview = dec_all[:, :].rearrange("p (a b i) -> p a b i", b=2, i=128)[
                            :, :, par2:par2 + 1, :]
                        oview = adT[k % 2][:, :].rearrange("p (a b i) -> p a b i", b=2, i=128)[
                            :, :, par2:par2 + 1, :]
                        tok_ad = cv.inc(ve.tensor_tensor(oview, aview, dview, OP.mult))
                    admul_rel[0] = tok_ad
                    rd_end(cv, tok_ad, R["P2a"], R["P2b"])

                def scan(k):
                    sc, c2 = divmod(k, 4)
                    p = sc % 2
                    gi = k
                    W(te, "t", cv, sbf_tok[0])
                    pe_begin(R["P3a"])
                    last = None
                    for h in range(8):
                        oc = h // 2
                        last = te.matmul(
                            P3a[:, h * 64:(h + 1) * 64],
                            q_sl[p][:, oc * 512 + c2 * 128: oc * 512 + (c2 + 1) * 128],
                            state_bf[:, h * 64:(h + 1) * 64], start=True, stop=True)
                    tok_mm = pe_end(last, R["P3a"])
                    mem_tok[0] = tok_mm
                    W(te, "t", cv, admul_rel[0])
                    pe_begin(R["P0a"])
                    last = None
                    for h in range(8):
                        last = te.matmul(
                            P0[:, h * 64:(h + 1) * 64],
                            adT[k % 2][:, h * 128:(h + 1) * 128],
                            vb_sl[p][:, c2 * GW + h * 64: c2 * GW + (h + 1) * 64],
                            start=True, stop=True)
                    tok_yi_mm = pe_end(last, R["P0a"])
                    adT_rel[k % 2] = tok_yi_mm
                    pe_begin(R["P8"])
                    last = None
                    for h in range(8):
                        poff = (h % 2) * 64
                        last = te.matmul(
                            P8[poff:poff + 64, (h // 2) * 64:(h // 2 + 1) * 64],
                            ktok[p][:, c2 * GW + h * 64: c2 * GW + (h + 1) * 64],
                            vbm_sl[p][:, c2 * GW + h * 64: c2 * GW + (h + 1) * 64],
                            start=True, stop=True)
                    tok_dl = pe_end(last, R["P8"])
                    slab_rel[p] = tok_dl
                    rd_begin(ve, "v", R["P8"])
                    W(ve, "v", ct, mem_tok[0])
                    W(ve, "v", cv, cv.v)
                    tok_su = None
                    for h in range(8):
                        poff = (h % 2) * 64
                        cv.inc(ve.scalar_tensor_tensor(
                            state_bf[poff:poff + 64, h * 64:(h + 1) * 64],
                            state[grp][poff:poff + 64, h * 64:(h + 1) * 64],
                            asumb_b[p][poff:poff + 64, c2 * 8 + h: c2 * 8 + h + 1],
                            P8[poff:poff + 64, (h // 2) * 64:(h // 2 + 1) * 64],
                            OP.mult, OP.add))
                    for h in range(8):
                        poff = (h % 2) * 64
                        tok_su = cv.inc(ve.scalar_tensor_tensor(
                            state[grp][poff:poff + 64, h * 64:(h + 1) * 64],
                            state[grp][poff:poff + 64, h * 64:(h + 1) * 64],
                            asumb_b[p][poff:poff + 64, c2 * 8 + h: c2 * 8 + h + 1],
                            P8[poff:poff + 64, (h // 2) * 64:(h // 2 + 1) * 64],
                            OP.mult, OP.add))
                    sbf_tok[0] = tok_su
                    rd_end(cv, tok_su, R["P8"])
                    rd_begin(se, "s", R["P0a"])
                    W(se, "s", cv, cv.v)
                    tok_yi = cs.inc(se.copy(yi_sb[:, :], P0[:, 0:512]))
                    rd_end(cs, tok_yi, R["P0a"])
                    W(ve, "v", cs, tok_yi)
                    W(ve, "v", cs, curve_tok[p])
                    rd_begin(ve, "v", R["P3a"])
                    mview = P3a[:, :].rearrange("p (h d) -> p h d", d=64)
                    cuv = curve_b[p][:, c2 * 8:(c2 + 1) * 8].rearrange("p (h d) -> p h d", d=1)
                    m0, m1 = bass.broadcast_tensor_aps(mview, cuv)
                    tok = cv.inc(ve.tensor_tensor(
                        tmc[:, :].rearrange("p (h d) -> p h d", d=64), m0, m1, OP.mult))
                    W(ve, "v", cv, tok)
                    Ws(ve, "v", s_t[c2 % 2], f"s_t{c2 % 2}", 16 * t_cnt[c2 % 2])
                    tok_ep = cv.inc(ve.tensor_add(ybf[c2 % 2][:, :], tmc[:, :], yi_sb[:, :]))
                    rd_end(cv, tok_ep, R["P3a"])
                    W(sy, "sy", cv, tok_ep)
                    sy.dma_start(y_tok[hb + gi * 128: hb + (gi + 1) * 128, gofs:gofs + GW],
                                 ybf[c2 % 2][:, :]).then_inc(s_t[c2 % 2], 16)
                    t_cnt[c2 % 2] += 1

                for _ in proj_gen(0):
                    pass
                smalls(0)
                for sc in range(NSC):
                    pg = proj_gen(sc + 1) if sc < NSC - 1 else iter(())

                    def pump(n):
                        for _ in range(n):
                            if next(pg, "D") == "D":
                                return

                    prep(4 * sc)
                    pump(3)
                    prep(4 * sc + 1)
                    pump(3)
                    scan(4 * sc)
                    pump(3)
                    prep(4 * sc + 2)
                    pump(3)
                    scan(4 * sc + 1)
                    pump(3)
                    prep(4 * sc + 3)
                    pump(3)
                    scan(4 * sc + 2)
                    pump(24)
                    if sc < NSC - 1:
                        smalls(sc + 1)
                    scan(4 * sc + 3)
                wa_rel[0] = (ct, ct.v)

        for half in range(2):
            hb = half * HALF
            barrier()
            x_load(0, hb)
            x_load(1, hb + 128)
            ln_tr_tok = [0, 0]
            ln_xnt_tok = [0, 0]
            for tc in range(NCH):
                par = tc % 2
                gc = half * NCH + tc
                xn_t = out_sb[par]
                x32 = Dsb if par == 0 else Dm_all
                x_wait(ve, "v", par)
                tok = cv.inc(ve.bn_stats(bnst[par][:, 0:6], x_sb[par][:, 0:512]))
                tok = cv.inc(ve.bn_stats(bnst[par][:, 6:12], x_sb[par][:, 512:1024]))
                W(ve, "v", cv, tok)
                tok = cv.inc(ve.bn_aggr(mv[par][:, :], bnst[par][:, :]))
                W(se, "s", cv, tok)
                tok_s = cs.inc(se.activation(stdv[par][:, :], mv[par][:, 1:2], AF.Sqrt,
                                             bias=c_eps[:, :]))
                W(ve, "v", cs, tok_s)
                tok = cv.inc(ve.reciprocal(rstd[par][:, :], stdv[par][:, :]))
                W(ve, "v", cv, tok)
                W(ve, "v", ct, ln_tr_tok[par])
                if o_rel[par] is not None:
                    W(ve, "v", o_rel[par][0], o_rel[par][1])
                Ws(ve, "v", s_o[par], f"s_o{par}", 16 * o_cnt[par])
                tok_xn = cv.inc(ve.tensor_scalar(xn_t[:, :], x_sb[par][:, :],
                                                 mv[par][:, 0:1], rstd[par][:, :],
                                                 OP.subtract, OP.mult))
                x_rel[par] = (cv, tok_xn)
                if tc + 2 < NCH:
                    x_load(par, hb + (tc + 2) * 128)
                W(te, "t", cv, tok_xn)
                regs_ = (R["P0a"], R["P0b"]) if par == 0 else (R["P2a"], R["P2b"])
                PT = P0 if par == 0 else P2
                pe_begin(*regs_)
                last = None
                for ic in range(IC):
                    last = te.transpose(PT[:, ic * 128:(ic + 1) * 128],
                                        xn_t[:, ic * 128:(ic + 1) * 128], ident32[:, :])
                pe_end(last, *regs_)
                rd_begin(ve, "v", *regs_)
                W(ve, "v", cv, ln_xnt_tok[par])
                tok = cv.inc(ve.tensor_copy(x32[:, 0:512], PT[:, 0:512]))
                tok32 = cv.inc(ve.tensor_copy(x32[:, 512:1024], PT[:, 512:1024]))
                rd_end(cv, tok32, *regs_)
                W(ve, "v", cv, tok32)
                tok_xnt = cv.inc(ve.tensor_copy(
                    xnT[:, :].rearrange("p (i t) -> p i t", t=HALF)[:, :, tc * 128:(tc + 1) * 128],
                    x32[:, :].rearrange("p (i t) -> p i t", t=128)))
                Ws(ve, "v", s_t[par], f"s_t{par}", 16 * t_cnt[par])
                tok_xb = cv.inc(ve.tensor_copy(xnbf[par][:, :], xn_t[:, :]))
                W(sy, "sy", cv, tok_xb)
                sy.dma_start(xn_tok[hb + tc * 128: hb + (tc + 1) * 128, :],
                             xnbf[par][:, :]).then_inc(s_t[par], 16)
                t_cnt[par] += 1
                W(te, "t", cv, tok32)
                greg = R["P3a"] if par == 0 else R["P8"]
                GP = P3a if par == 0 else P8
                pe_begin(greg)
                last = None
                for ic in range(IC):
                    last = te.matmul(GP[:, 0:8], x32[:, ic * 128:(ic + 1) * 128],
                                     wred8[:, ic * 8:(ic + 1) * 8],
                                     start=(ic == 0), stop=(ic == IC - 1))
                tok_g = pe_end(last, greg)
                ln_tr_tok[par] = tok_g
                rd_begin(ve, "v", greg)
                tok = cv.inc(ve.tensor_copy(lg[par][:, :], GP[:, 0:8]))
                rd_end(cv, tok, greg)
                ln_xnt_tok[par] = tok if False else ln_xnt_tok[par]
                W(ve, "v", cv, tok)
                tok = cv.inc(ve.reduce_max(ngmx[par][:, :], lg[par][:, :], axis=AX.X,
                                           negate=True))
                W(se, "s", cv, tok)
                tok_e = cs.inc(se.activation(expd[par][:, :], lg[par][:, :], AF.Exp,
                                             bias=ngmx[par][:, :]))
                W(ve, "v", cs, tok_e)
                tok = cv.inc(ve.reduce_sum(ssum[par][:, :], expd[par][:, :], axis=AX.X))
                W(ve, "v", cv, tok)
                tok = cv.inc(ve.reciprocal(score_sl[:, gc:gc + 1], ssum[par][:, :]))
                tok = cv.inc(ve.max(mx8[par][:, :], lg[par][:, :]))
                W(ve, "v", cv, tok)
                tok = cv.inc(ve.max_index(ix8[par][:, :], mx8[par][:, :], lg[par][:, :]))
                W(ve, "v", cv, tok)
                tok = cv.inc(ve.tensor_copy(idx_sl[:, gc:gc + 1], ix8[par][:, 0:1]))
                W(ve, "v", cv, tok)
                for e in range(E):
                    tok_m = cv.inc(ve.tensor_single_scalar(
                        msk_sl[:, gc * E + e:gc * E + e + 1], idx_sl[:, gc:gc + 1],
                        float(e), OP.is_equal))
                ln_xnt_tok[par] = tok_m

            attn_phase(half)

        barrier()
        ov_a.close()
        ov_b = ExitStack()
        bsb = lambda name, shape, dt=F32: ov_b.enter_context(nc.sbuf_tensor(name, list(shape), dt))
        w1e = [bsb(f"w1e{i}", (128, IC * 512), BF16) for i in range(3)]
        w2e = [bsb(f"w2e{i}", (128, HCE * 1024), BF16) for i in range(3)]
        h_sb = [bsb(f"h_sb{i}", (128, HCE * 512), BF16) for i in range(2)]
        b1c = bsb("p_b1c", (128, 256))
        b2all = bsb("p_b2all", (1, E * D), BF16)
        xnT_pe = [bsb(f"xnT_pe{i}", (128, IC * CAP), BF16) for i in range(2)]
        ffn_acc = bsb("ffn_acc", (128, NST * D), BF16)
        xperm_t = [bsb(f"xperm_t{i}", (128, D), BF16) for i in range(4)]
        ybuf = [bsb(f"ybuf{i}", (128, D), BF16) for i in range(2)]
        fbuf = [bsb(f"fbuf{i}", (128, D), BF16) for i in range(2)]
        barrier()
        sy.dma_start(b1c[:, :], ap["b1_c"][:, :]).then_inc(s_pre, 16)
        npre += 16
        Ws(se, "s", s_pre, "s_pre", npre)

        for c in range(2 * NCH):
            pe_begin(R["P8"])
            pe_end(te.matmul(P8[0:8, 0:1], msk_sl[:, c * E:(c + 1) * E],
                             onescol_bf[:, :], start=True, stop=True), R["P8"])
            rd_begin(ve, "v", R["P8"])
            tok = cv.inc(ve.tensor_copy(tot_sb[:, c:c + 1], P8[0:8, 0:1]))
            rd_end(cv, tok, R["P8"])
        W(te, "t", cv, cv.v)
        pe_begin(R["P8"])
        pe_end(te.transpose(P8[0:32, 8:16], tot_sb[:, :], ident32[0:8, 0:8]), R["P8"])
        rd_begin(ve, "v", R["P8"])
        tok = cv.inc(ve.tensor_copy(asb[0:32, 0:8], P8[0:32, 8:16]))
        rd_end(cv, tok, R["P8"])
        W(te, "t", cv, tok)
        pe_begin(R["P8"])
        pe_end(te.matmul(P8[0:32, 16:24], u1x32[:, :], asb[0:32, 0:8],
                         start=True, stop=True), R["P8"])
        rd_begin(ve, "v", R["P8"])
        tok = cv.inc(ve.tensor_copy(cb_sb[:, :], P8[0:32, 16:24]))
        rd_end(cv, tok, R["P8"])
        pos_tok = 0
        for c in range(2 * NCH):
            W(te, "t", cv, cv.v)
            pe_begin(R["P8"])
            te.matmul(P8[:, 24:32], u1bf[:, :], msk_sl[:, c * E:(c + 1) * E],
                      start=True, stop=True)
            pe_end(te.matmul(P8[0:1, 32:40], ident32[0:32, c:c + 1], cb_sb[:, :],
                             start=True, stop=True), R["P8"])
            rd_begin(ve, "v", R["P8"])
            tok = cv.inc(ve.tensor_copy(asb[:, :], P8[:, 24:32]))
            tok = cv.inc(ve.tensor_copy(rowsb[:, :], P8[0:1, 32:40]))
            rd_end(cv, tok, R["P8"])
            W(ve, "v", cv, tok)
            tok_row = cv.inc(ve.tensor_add(rowsb[:, :], rowsb[:, :], gbase_r[:, :]))
            W(te, "t", cv, tok_row)
            pe_begin(R["P8"])
            pe_end(te.matmul(P8[:, 40:48], ones_r[:, :], rowsb[:, :],
                             start=True, stop=True), R["P8"])
            W(ve, "v", cv, tok_row)
            tok = cv.inc(ve.tensor_tensor(t3sb[:, :], msk_sl[:, c * E:(c + 1) * E],
                                          asb[:, :], OP.mult))
            W(ve, "v", cv, tok)
            tok = cv.inc(ve.reduce_sum(rk_sb[:, :], t3sb[:, :], axis=AX.X))
            W(ve, "v", cv, tok)
            tok = cv.inc(ve.tensor_scalar(ov_sb[:, :], rk_sb[:, :], float(CAP), 1e9,
                                          OP.is_gt, OP.mult))
            rd_begin(ve, "v", R["P8"])
            tok = cv.inc(ve.tensor_add(absb[:, :], asb[:, :], P8[:, 40:48]))
            rd_end(cv, tok, R["P8"])
            W(ve, "v", cv, tok)
            tok = cv.inc(ve.tensor_tensor(t3sb[:, :], msk_sl[:, c * E:(c + 1) * E],
                                          absb[:, :], OP.mult))
            W(ve, "v", cv, tok)
            tok = cv.inc(ve.reduce_sum(posf[:, :], t3sb[:, :], axis=AX.X))
            W(ve, "v", cv, tok)
            tok = cv.inc(ve.scalar_tensor_tensor(posf[:, :], posf[:, :], -1.0,
                                                 ov_sb[:, :], OP.add, OP.add))
            W(ve, "v", cv, tok)
            pos_tok = cv.inc(ve.tensor_copy(pos_sl[:, c:c + 1], posf[:, :]))
        tok_z = cv.inc(ve.memset(ffn_acc[:, :], 0.0))
        W(sy, "sy", cv, tok_z)
        for e8 in range(E):
            sy.dma_start(
                xn_perm[e8 * CAP:(e8 + 1) * CAP, :].rearrange("(t p) c -> p t c", p=128),
                ffn_acc[:, :].rearrange("p (t c) -> p t c", c=D)).then_inc(s_t[e8 % 2], 16)
            t_cnt[e8 % 2] += 1
        for p_ in range(2):
            Ws(gp, "gp", s_t[p_], f"s_t{p_}", 16 * t_cnt[p_])
        W(gp, "gp", cv, pos_tok)
        for c in range(2 * NCH):
            par = c % 4
            if c >= 4:
                Ws(sy, "sy", s_gs[par], f"s_gs{par}", 16 * gs_cnt[par])
            sy.dma_start(xperm_t[par][:, :],
                         xn_tok[c * 128:(c + 1) * 128, :]).then_inc(s_r[par], 16)
            r_cnt[par] += 1
            Ws(gp, "gp", s_r[par], f"s_r{par}", 16 * r_cnt[par])
            gp.indirect_dma_start(
                xn_perm[:, :], bass.IndirectOffsetOnAxis(pos_sl[:, c:c + 1], 0),
                xperm_t[par][:, :], None,
                bounds_check=bc_reg, oob_is_err=False).then_inc(s_gs[par], 16)
            gs_cnt[par] += 1

        def w_load(it):
            e_, we_ = divmod(it, NWE)
            par = it % 3
            if w_rel[par] is not None:
                W(gp, "gp", w_rel[par][0], w_rel[par][1])
            gp.dma_start(
                w1e[par][:, :].rearrange("p (i h) -> p i h", h=512),
                ap["w1"][e_, :, we_ * 512:(we_ + 1) * 512].rearrange(
                    "(i p) h -> p i h", p=128)).then_inc(s_w[par], 16)
            gp.dma_start(
                w2e[par][:, :].rearrange("p (i h) -> p i h", h=1024),
                ap["w2"][e_, we_ * 512:(we_ + 1) * 512, :].rearrange(
                    "(i p) h -> p i h", p=128)).then_inc(s_w[par], 16)
            w_cnt[par] += 2

        tr_units = []
        xp_rel = [0, 0, 0, 0]
        xnTpe_rel = [0, 0]
        load_cnt = {}

        def emit_tile_load(e_, st):
            par = (e_ * NST + st) % 4
            pv = xp_rel[par]
            if pv:
                W(sy, "sy", ct, pv)
            Ws(sy, "sy", s_g[0], "s_g0", 16 * g_cnt[0])
            Ws(sy, "sy", s_g[1], "s_g1", 16 * g_cnt[1])
            for i4 in range(4):
                Ws(sy, "sy", s_gs[i4], f"s_gs{i4}", 16 * gs_cnt[i4])
            sy.dma_start(xperm_t[par][:, :],
                         xn_perm[e_ * CAP + st * 128: e_ * CAP + (st + 1) * 128, :]
                         ).then_inc(s_r[par], 16)
            r_cnt[par] += 1
            load_cnt[(e_, st)] = r_cnt[par]

        def make_tr_unit(e_, st, halfu):
            par = (e_ * NST + st) % 4
            pe = e_ % 2

            def emit():
                Ws(te, "t", s_r[par], f"s_r{par}", 16 * load_cnt[(e_, st)])
                pe_begin(R["P3a"])
                last = None
                for j in range(4):
                    ic = halfu * 4 + j
                    last = te.matmul(P3a[:, j * 128:(j + 1) * 128],
                                     xperm_t[par][:, ic * 128:(ic + 1) * 128],
                                     identbf[:, :], start=True, stop=True)
                tok_t = pe_end(last, R["P3a"])
                if halfu == 1:
                    xp_rel[par] = tok_t
                rd_begin(ve, "v", R["P3a"])
                W(ve, "v", ct, xnTpe_rel[pe])
                dst = xnT_pe[pe][:, :].rearrange("p (i t) -> p i t", t=CAP)
                tok = cv.inc(ve.tensor_copy(
                    dst[:, halfu * 4:(halfu + 1) * 4, st * 128:(st + 1) * 128],
                    P3a[:, :].rearrange("p (j t) -> p j t", t=128)))
                rd_end(cv, tok, R["P3a"])
                if halfu == 1 and st + 4 < NST:
                    emit_tile_load(e_, st + 4)
            return emit

        def queue_expert_transposes(e_):
            for st0 in range(4):
                emit_tile_load(e_, st0)
            for st in range(NST):
                for halfu in range(2):
                    tr_units.append(make_tr_unit(e_, st, halfu))

        def pump_tr(n=1):
            for _ in range(n):
                if tr_units:
                    tr_units.pop(0)()

        barrier(("gp",))
        gp.dma_start(b2all[:, :], ap["b2r"][:, :]).then_inc(s_g[0], 16)
        g_cnt[0] += 1
        w_load(0)
        w_load(1)
        queue_expert_transposes(0)
        pump_tr(len(tr_units))
        h_rel = [None, None]
        ffn_w_tok = [0, 0]
        for it in range(E * NWE):
            e_, we_ = divmod(it, NWE)
            par = it % 3
            pe = e_ % 2
            if it + 2 < E * NWE:
                w_load(it + 2)
            if we_ == 0:
                if e_ + 1 < E:
                    queue_expert_transposes(e_ + 1)
            Ws(te, "t", s_w[par], f"s_w{par}", 16 * w_cnt[par])
            Ws(te, "t", s_g[0], "s_g0", 16 * g_cnt[0])
            for tb in range(NTB3):
                hp = tb % 2
                gelu_last = None
                for hc in range(HCE):
                    hreg = R["P0a"] if hc % 2 == 0 else R["P0b"]
                    cshift = 0 if hc % 2 == 0 else 512
                    if h_rel[hp] is not None:
                        W(se, "s", h_rel[hp][0], h_rel[hp][1])
                    pe_begin(hreg)
                    last = None
                    for ic in range(IC):
                        last = te.matmul(
                            P0[:, cshift:cshift + 512],
                            w1e[par][:, ic * 512 + hc * 128: ic * 512 + (hc + 1) * 128],
                            xnT_pe[pe][:, ic * CAP + tb * 512: ic * CAP + (tb + 1) * 512],
                            start=(ic == 0), stop=(ic == IC - 1))
                    pe_end(last, hreg)
                    rd_begin(se, "s", hreg)
                    gelu_last = cs.inc(se.activation(
                        h_sb[hp][:, hc * 512:(hc + 1) * 512], P0[:, cshift:cshift + 512],
                        AF.Gelu,
                        bias=b1c[:, e_ * 32 + we_ * 4 + hc: e_ * 32 + we_ * 4 + hc + 1]))
                    rd_end(cs, gelu_last, hreg)
                W(te, "t", cs, gelu_last)
                for tc2 in range(4):
                    tcg = tb * 4 + tc2
                    ra, rb = (R["P1a"], R["P1b"]) if tcg % 2 == 0 else (R["P2a"], R["P2b"])
                    PT = P1 if tcg % 2 == 0 else P2
                    pe_begin(ra, rb)
                    last = None
                    for hc in range(HCE):
                        for cb in range(2):
                            last = te.matmul(
                                PT[:, cb * 512:(cb + 1) * 512],
                                h_sb[hp][:, hc * 512 + tc2 * 128: hc * 512 + (tc2 + 1) * 128],
                                w2e[par][:, hc * 1024 + cb * 512: hc * 1024 + (cb + 1) * 512],
                                start=(hc == 0),
                                stop=(hc == HCE - 1 and we_ != 0))
                    if we_ == 0:
                        for cb in range(2):
                            last = te.matmul(PT[:, cb * 512:(cb + 1) * 512], ones_bf[0:1, 0:128],
                                             b2all[0:1, e_ * D + cb * 512: e_ * D + (cb + 1) * 512],
                                             start=False, stop=True)
                    pe_end(last, ra, rb)
                    rd_begin(ve, "v", ra, rb)
                    if we_ == 0:
                        Ws(ve, "v", s_t[0], "s_t0", 16 * t_cnt[0])
                        Ws(ve, "v", s_t[1], "s_t1", 16 * t_cnt[1])
                        Ws(ve, "v", s_fp, "s_fp", 16 * fp_cnt[0])
                        tok = cv.inc(ve.tensor_copy(
                            ffn_acc[:, tcg * D:(tcg + 1) * D], PT[:, :]))
                    else:
                        W(ve, "v", cv, cv.v)
                        tok = cv.inc(ve.tensor_tensor(
                            ffn_acc[:, tcg * D:(tcg + 1) * D],
                            ffn_acc[:, tcg * D:(tcg + 1) * D], PT[:, :], OP.add))
                    rd_end(cv, tok, ra, rb)
                    if we_ == NWE - 1:
                        ffn_w_tok[pe] = tok
                    pump_tr(1)
                h_rel[hp] = (ct, ct.v)
            w_rel[par] = (ct, ct.v)
            if we_ == NWE - 1:
                xnTpe_rel[pe] = ct.v
            if we_ == NWE - 1:
                W(gp, "gp", cv, ffn_w_tok[pe])
                gp.dma_start(
                    ffn_perm[e_ * CAP:(e_ + 1) * CAP, :].rearrange("(t p) c -> p t c", p=128),
                    ffn_acc[:, :].rearrange("p (t c) -> p t c", c=D)
                    ).then_inc(s_fp, 16)
                fp_cnt[0] += 1

        for p_ in range(2):
            Ws(gp, "gp", s_t[p_], f"s_t{p_}", 16 * t_cnt[p_])
            Ws(sy, "sy", s_t[p_], f"s_t{p_}", 16 * t_cnt[p_])
        Ws(gp, "gp", s_fp, "s_fp", 16 * fp_cnt[0])
        Ws(sy, "sy", s_fp, "s_fp", 16 * fp_cnt[0])
        x_rel = [None, None]
        for c in range(2 * NCH):
            par = c % 2
            if x_rel[par] is not None:
                W(sy, "sy", x_rel[par][0], x_rel[par][1])
            sy.dma_start(x_sb[par][:, :], ap["x"][c * 128:(c + 1) * 128, :]
                         ).then_inc(s_x[par], 16)
            x_cnt[par] += 1
            Ws(sy, "sy", s_o[par], f"s_o{par}", 16 * o_cnt[par])
            sy.dma_start(ybuf[par][:, :], y_tok[c * 128:(c + 1) * 128, :]
                         ).then_inc(s_r[par], 16)
            r_cnt[par] += 1
            Ws(gp, "gp", s_o[par], f"s_o{par}", 16 * o_cnt[par])
            gp.indirect_dma_start(
                fbuf[par][:, :], None,
                ffn_perm[:, :], bass.IndirectOffsetOnAxis(pos_sl[:, c:c + 1], 0),
                bounds_check=bc_reg, oob_is_err=False).then_inc(s_g[par], 16)
            g_cnt[par] += 1
            Ws(ve, "v", s_x[par], f"s_x{par}", 16 * x_cnt[par])
            Ws(ve, "v", s_r[par], f"s_r{par}", 16 * r_cnt[par])
            Ws(ve, "v", s_g[par], f"s_g{par}", 16 * g_cnt[par])
            if o_rel[par] is not None:
                W(ve, "v", o_rel[par][0], o_rel[par][1])
            Ws(ve, "v", s_o[par], f"s_o{par}", 16 * o_cnt[par])
            W(ve, "v", cv, cv.v)
            tok = cv.inc(ve.tensor_add(out_sb[par][:, :], x_sb[par][:, :], ybuf[par][:, :]))
            W(ve, "v", cv, tok)
            tok_o = cv.inc(ve.scalar_tensor_tensor(
                out_sb[par][:, :], fbuf[par][:, :], score_sl[:, c:c + 1],
                out_sb[par][:, :], OP.mult, OP.add))
            x_rel[par] = (cv, tok_o)
            o_rel[par] = (cv, tok_o)
            W(sy, "sy", cv, tok_o)
            sy.dma_start(out_d[c * 128:(c + 1) * 128, :], out_sb[par][:, :]
                         ).then_inc(s_o[par], 16)
            o_cnt[par] += 1
        ov_b.close()

        Ws(te, "t", s_o[0], "s_o0", 16 * o_cnt[0])
        Ws(te, "t", s_o[1], "s_o1", 16 * o_cnt[1])
    return nc


from concourse.bass_utils import run_bass_kernel_spmd

_NCORES = 8


def kernel(**inputs):
    nc = build(interleave=False)
    in_maps = [host_inputs(inputs, c) for c in range(_NCORES)]
    res = run_bass_kernel_spmd(nc, in_maps, core_ids=list(range(_NCORES)))
    return np.stack([np.asarray(res.results[c]["out"]) for c in range(_NCORES)], axis=0)


# version 23 (final)
# speedup vs baseline: 1.0610x; 1.0026x over previous
import numpy as np
import ml_dtypes
from contextlib import ExitStack
import concourse.bass as bass
import concourse.mybir as mybir

F32 = mybir.dt.float32
BF16 = mybir.dt.bfloat16
U32 = mybir.dt.uint32
U8 = mybir.dt.uint8
AF = mybir.ActivationFunctionType
OP = mybir.AluOpType
AX = mybir.AxisListType

T, D, HALF = 4096, 1024, 2048
NCH = 16
IC = 8
E, HID = 8, 4096
NWE, HCE, NTB = 8, 4, 4
NG, GW, HD = 2, 512, 64
CAP = 1536
NSLOT = E * CAP
NST = CAP // 128
NTB3 = CAP // 512
SCALE = HD ** -0.5


def host_inputs(full, core):
    f32 = np.float32
    bf = ml_dtypes.bfloat16
    return {
        "x": np.ascontiguousarray(full["L0_attn_x"][core]),
        "wqT": np.ascontiguousarray(full["L0_attn_Wq"].T),
        "wkT": np.ascontiguousarray(full["L0_attn_Wk"].T),
        "wvT": np.ascontiguousarray(full["L0_attn_Wv"].T),
        "wbT": np.ascontiguousarray(full["L0_attn_Wbeta"].T),
        "waT": np.ascontiguousarray(full["L0_attn_Walpha"].T),
        "bq_c": np.ascontiguousarray(full["L0_attn_bq"].reshape(IC, 128).T),
        "bk_c": np.ascontiguousarray(full["L0_attn_bk"].reshape(IC, 128).T),
        "bk_r": np.ascontiguousarray(full["L0_attn_bk"].reshape(1, D)),
        "bq_r": np.ascontiguousarray(full["L0_attn_bq"].reshape(1, D)),
        "bv_r": np.ascontiguousarray(full["L0_attn_bv"].reshape(1, D)),
        "bb_r": np.ascontiguousarray(full["L0_attn_bbeta"].reshape(1, D)),
        "ba_r": np.ascontiguousarray(full["L0_attn_balpha"].reshape(1, D)),
        "wgred": full["L0_ffn_wg_red"],
        "wg": full["L0_ffn_wg"],
        "w1": full["L0_ffn_w1"],
        "b1_c": np.ascontiguousarray(
            full["L0_ffn_b1"].reshape(E, 32, 128).transpose(2, 0, 1).reshape(128, 256)),
        "w2": full["L0_ffn_w2"],
        "b2": full["L0_ffn_b2"],
        "ident32": np.eye(128, dtype=f32),
        "u64": np.ascontiguousarray((np.triu(np.ones((128, 128), f32)) / 64.0).astype(f32)),
        "u1bf": np.ascontiguousarray(np.triu(np.ones((128, 128), bf))),
        "u1x32": np.ascontiguousarray(np.triu(np.ones((32, 32), f32), k=1)),
        "onescol_bf": np.ones((128, 1), bf),
        "gbase": (np.arange(8, dtype=f32) * 1536.0).reshape(1, 8),
        "b2r": np.ascontiguousarray(full["L0_ffn_b2"].reshape(1, 8 * 1024)),
        "identbf": np.eye(128, dtype=bf),
        "triu8": np.ascontiguousarray(np.tile(np.triu(np.ones((128, 128), np.uint8)), (1, 8))),
        "onehot8": np.ascontiguousarray(
            np.concatenate([np.eye(8, dtype=f32)[:, h:h + 1] * np.ones((1, 128), f32)
                            for h in range(8)], axis=1)),
        "posbig": np.full((128, 1024), 1e9, f32),
        "ones_r": np.ones((1, 128), f32),
        "ones_bf": np.ones((1, 512), bf),
    }


class Ctr:
    def __init__(self, es, nc, name):
        self.sem = es.enter_context(nc.semaphore(name))
        self.v = 0
        self.name = name

    def inc(self, ins, n=1):
        ins.then_inc(self.sem, n)
        self.v += n
        return self.v


class Reg:
    def __init__(self, name):
        self.name = name
        self.w = None
        self.r = {}


def build(interleave=True, dbg=False):
    nc = bass.Bass("TRN2", target_bir_lowering=False)
    ap = {}

    def din(name, shape, dt=F32):
        ap[name] = nc.dram_tensor(name, list(shape), dt, kind="ExternalInput").ap()
        return ap[name]

    din("x", (T, D))
    for n in ("wqT", "wkT", "wvT", "wbT", "waT"):
        din(n, (D, D))
    din("bq_c", (128, IC)); din("bk_c", (128, IC))
    din("bk_r", (1, D)); din("bq_r", (1, D))
    din("bv_r", (1, D)); din("bb_r", (1, D)); din("ba_r", (1, D))
    din("wgred", (16, D)); din("wg", (E, 16))
    din("w1", (E, D, HID)); din("b1_c", (128, 256))
    din("w2", (E, HID, D)); din("b2", (E, D))
    din("ident32", (128, 128))
    din("u64", (128, 128)); din("triu8", (128, 1024), U8)
    din("u1bf", (128, 128), BF16); din("u1x32", (32, 32)); din("onescol_bf", (128, 1), BF16)
    din("gbase", (1, 8)); din("b2r", (1, 8 * 1024)); din("identbf", (128, 128), BF16)
    din("onehot8", (8, 8 * 128)); din("posbig", (128, 1024))
    din("ones_r", (1, 128)); din("ones_bf", (1, 512), BF16)

    out_d = nc.dram_tensor("out", [T, D], F32, kind="ExternalOutput").ap()
    xn_tok = nc.dram_tensor("xn_tok", [T, D], BF16, kind="Internal").ap()
    xn_perm = nc.dram_tensor("xn_perm", [NSLOT, D], BF16, kind="Internal").ap()
    ffn_perm = nc.dram_tensor("ffn_perm", [NSLOT, D], BF16, kind="Internal").ap()
    y_tok = nc.dram_tensor("y_tok", [T, D], BF16, kind="Internal").ap()

    with ExitStack() as es:
        sb = lambda name, shape, dt=F32: es.enter_context(nc.sbuf_tensor(name, list(shape), dt))
        sem = lambda n: es.enter_context(nc.semaphore(n))
        ct, cv, cs = Ctr(es, nc, "ct"), Ctr(es, nc, "cv"), Ctr(es, nc, "cs")
        te, ve, se, sy, gp = nc.tensor, nc.vector, nc.scalar, nc.sync, nc.gpsimd
        waited = {}

        def W(eng, ename, ctr, tok):
            if tok is None or tok <= 0:
                return
            key = (ename, ctr.name)
            if waited.get(key, 0) >= tok:
                return
            eng.wait_ge(ctr.sem, tok)
            waited[key] = tok

        def Ws(eng, ename, semh, sname, tok):
            if tok <= 0:
                return
            key = (ename, sname)
            if waited.get(key, 0) >= tok:
                return
            eng.wait_ge(semh, tok)
            waited[key] = tok

        def pe_begin(*rs):
            for reg in rs:
                for (ctr_, tok_) in reg.r.values():
                    W(te, "t", ctr_, tok_)
                reg.r = {}

        def pe_end(last_ins, *rs):
            tok = ct.inc(last_ins)
            for reg in rs:
                reg.w = (ct, tok)
            return tok

        def rd_begin(eng, ename, *rs):
            for reg in rs:
                if reg.w is not None:
                    W(eng, ename, reg.w[0], reg.w[1])

        def rd_end(ctr, tok, *rs):
            for reg in rs:
                reg.r[ctr.name] = (ctr, tok)

        score_sl = sb("score_sl", (128, 2 * NCH))
        idx_sl = sb("idx_sl", (128, 2 * NCH))
        pos_sl = sb("pos_sl", (128, 2 * NCH), U32)
        msk_sl = sb("msk_sl", (128, 2 * NCH * E), BF16)
        state = [sb(f"state{g}", (128, 512)) for g in range(NG)]
        state_bf = sb("state_bf", (128, 512), BF16)
        x_sb = [sb(f"x_sb{i}", (128, D)) for i in range(2)]
        out_sb = [sb(f"out_sb{i}", (128, D)) for i in range(2)]
        xn_sb = out_sb[0]
        xnT32 = out_sb[1]
        ident32 = sb("c_id32", (128, 128))
        identbf = sb("c_idbf", (128, 128), BF16)
        u64 = sb("c_u64", (128, 128))
        u1bf = sb("c_u1bf", (128, 128), BF16)
        u1x32 = sb("c_u1x32", (32, 32))
        onescol_bf = sb("c_ocbf", (128, 1), BF16)
        gbase_r = sb("c_gbase", (1, 8))
        triu8 = sb("c_triu", (128, 1024), U8)
        onehot8 = sb("c_oh8", (8, 8 * 128))
        posbig = sb("c_posbig", (128, 1024))
        ones_r = sb("c_ones", (1, 128))
        ones_bf = sb("c_onesbf", (1, 512), BF16)
        c_eps = sb("c_eps", (128, 1)); c_one = sb("c_one", (128, 1))
        bq_c = sb("p_bqc", (128, IC)); bk_c = sb("p_bkc", (128, IC))
        bqs = sb("p_bqs", (128, IC)); bks = sb("p_bks", (128, IC))
        brow4 = [sb(f"p_brow{i}", (1, GW), BF16) for i in range(4)]
        wred8 = sb("p_wred8", (128, IC * E))
        bnst = [sb(f"g_bnst{i}", (128, 12)) for i in range(2)]
        mv = [sb(f"g_mv{i}", (128, 2)) for i in range(2)]
        stdv = [sb(f"g_std{i}", (128, 1)) for i in range(2)]
        rstd = [sb(f"g_rstd{i}", (128, 1)) for i in range(2)]
        lg = [sb(f"g_lg{i}", (128, 8)) for i in range(2)]
        mx8 = [sb(f"g_mx8{i}", (128, 8)) for i in range(2)]
        ix8 = [sb(f"g_ix8{i}", (128, 8), U32) for i in range(2)]
        expd = [sb(f"g_expd{i}", (128, 8)) for i in range(2)]
        ngmx = [sb(f"g_ngmx{i}", (128, 1)) for i in range(2)]
        ssum = [sb(f"g_ssum{i}", (128, 1)) for i in range(2)]
        tot_sb = sb("pm_tot", (8, 32)); cb_sb = sb("pm_cb", (32, 8))
        rowsb = sb("pm_row", (1, 8)); asb = sb("pm_a", (128, 8)); absb = sb("pm_ab", (128, 8))
        t3sb = sb("pm_t3", (128, 8))
        rk_sb = sb("pm_rk", (128, 1)); ov_sb = sb("pm_ov", (128, 1)); posf = sb("pm_posf", (128, 1))

        ov_a = ExitStack()
        asb_ = lambda name, shape, dt=F32: ov_a.enter_context(nc.sbuf_tensor(name, list(shape), dt))
        xnT = asb_("xnT", (128, IC * HALF), BF16)
        slab = {n: asb_(f"at_{n}", (128, IC * GW), BF16)
                for n in ("wq", "wk", "wv", "wb", "wa")}
        q_sl = [asb_(f"q_sl{i}", (128, 2048), BF16) for i in range(2)]
        k_sl = [asb_(f"k_sl{i}", (128, 2048), BF16) for i in range(2)]
        v_sl = asb_("v_sl", (128, 512), BF16)
        vb_sl = [asb_(f"vb_sl{i}", (128, 2048), BF16) for i in range(2)]
        vbm_sl = [asb_(f"vbm_sl{i}", (128, 2048), BF16) for i in range(2)]
        ktok = [asb_(f"ktok{i}", (128, 2048), BF16) for i in range(2)]
        adT = [asb_(f"adT{i}", (128, 1024), BF16) for i in range(2)]
        browq = asb_("browq", (1, GW), BF16)
        browk = asb_("browk", (1, GW), BF16)
        L1 = asb_("L1", (128, GW)); CSb = asb_("CSb", (128, GW))
        NEG = asb_("NEG", (128, GW)); M2 = asb_("M2", (128, GW))
        BSP = asb_("BSP", (128, GW)); JUNK = asb_("JUNK", (128, GW))
        Dsb = asb_("Dsb", (128, 1024)); Dm_all = asb_("Dm_all", (128, 1024))
        dec_all = asb_("dec_all", (128, 1024))
        yi_sb = asb_("yi_sb", (128, GW)); t1_sb = asb_("t1_sb", (128, GW))
        tmc = asb_("tmc", (128, GW))
        lasum = [asb_(f"a_lasum{i}", (128, 32)) for i in range(2)]
        bmsum = asb_("a_bmsum", (128, 8))
        cums_b = [asb_(f"a_cums{i}", (128, 32)) for i in range(2)]
        cumsT_b = [asb_(f"a_cumsT{i}", (8, 512)) for i in range(2)]
        curve_b = [asb_(f"a_curve{i}", (128, 32)) for i in range(2)]
        asumb_b = [asb_(f"a_asumb{i}", (128, 32)) for i in range(2)]
        xnbf = [asb_(f"xnbf{i}", (128, D), BF16) for i in range(2)]
        ybf = [asb_(f"ybf{i}", (128, GW), BF16) for i in range(2)]
        st_ov = ExitStack()
        ssb = lambda name, shape, dt=F32: st_ov.enter_context(nc.sbuf_tensor(name, list(shape), dt))
        wg_sb = ssb("p_wg", (E, 16)); wgred_sb = ssb("p_wgred", (16, 512))
        wguT = ssb("p_wguT", (16, E)); wred8T = ssb("p_wred8T", (E, 512))
        wgsq = ssb("p_wgsq", (E, 16))
        nrm = ssb("p_nrm", (E, 1)); nrm2 = ssb("p_nrm2", (E, 1)); rinv = ssb("p_rinv", (E, 1))
        wgu = ssb("p_wgu", (E, 16))

        P0 = es.enter_context(nc.psum_tensor("P0", [128, 1024], F32))
        P1 = es.enter_context(nc.psum_tensor("P1", [128, 1024], F32))
        P2 = es.enter_context(nc.psum_tensor("P2", [128, 1024], F32))
        P3a = es.enter_context(nc.psum_tensor("P3a", [128, 512], F32))
        P8 = es.enter_context(nc.psum_tensor("P8", [128, 512], F32))
        R = {n: Reg(n) for n in ("P0a", "P0b", "P1a", "P1b", "P2a", "P2b", "P3a", "P8")}

        s_pre = sem("s_pre")
        s_x = [sem("s_x0"), sem("s_x1")]
        x_cnt = [0, 0]; x_rel = [None, None]
        s_w = [sem("s_w0"), sem("s_w1"), sem("s_w2")]
        s_fp = sem("s_fp")
        w_cnt = [0, 0, 0]; w_rel = [None, None, None]; fp_cnt = [0]
        s_wa = sem("s_wa")
        wa_cnt = [0]; wa_rel = [None]
        s_o = [sem("s_o0"), sem("s_o1")]
        o_cnt = [0, 0]; o_rel = [None, None]
        s_t = [sem("s_t0"), sem("s_t1")]
        t_cnt = [0, 0]
        s_r = [sem("s_r0"), sem("s_r1"), sem("s_r2"), sem("s_r3")]
        r_cnt = [0, 0, 0, 0]
        s_g = [sem("s_g0"), sem("s_g1")]
        g_cnt = [0, 0]
        bc_reg = es.enter_context(gp.register("bc"))
        gp.reg_mov(bc_reg, NSLOT - 1)

        def barrier(engines=("t", "v", "s", "sy", "gp")):
            objs = {"t": te, "v": ve, "s": se, "sy": sy, "gp": gp}
            for en in engines:
                eng = objs[en]
                for ctr in (ct, cv, cs):
                    W(eng, en, ctr, ctr.v)
                for p_ in range(2):
                    Ws(eng, en, s_o[p_], f"s_o{p_}", 16 * o_cnt[p_])
                    Ws(eng, en, s_t[p_], f"s_t{p_}", 16 * t_cnt[p_])
                    Ws(eng, en, s_r[p_], f"s_r{p_}", 16 * r_cnt[p_])
                    Ws(eng, en, s_g[p_], f"s_g{p_}", 16 * g_cnt[p_])
                    Ws(eng, en, s_x[p_], f"s_x{p_}", 16 * x_cnt[p_])
                for p_ in range(3):
                    Ws(eng, en, s_w[p_], f"s_w{p_}", 16 * w_cnt[p_])
                Ws(eng, en, s_wa, "s_wa", 16 * wa_cnt[0])
                Ws(eng, en, s_fp, "s_fp", 16 * fp_cnt[0])

        def x_load(par, row0):
            if x_rel[par] is not None:
                W(sy, "sy", x_rel[par][0], x_rel[par][1])
            sy.dma_start(x_sb[par][:, :], ap["x"][row0:row0 + 128, :]).then_inc(s_x[par], 16)
            x_cnt[par] += 1

        def x_wait(eng, ename, par):
            Ws(eng, ename, s_x[par], f"s_x{par}", 16 * x_cnt[par])

        npre = 0
        for dst, srcn in ((ident32, "ident32"), (identbf, "identbf"), (u64, "u64"),
                          (u1bf, "u1bf"), (u1x32, "u1x32"), (onescol_bf, "onescol_bf"),
                          (gbase_r, "gbase"),
                          (triu8, "triu8"), (posbig, "posbig"), (onehot8, "onehot8"),
                          (ones_r, "ones_r"), (ones_bf, "ones_bf"),
                          (bq_c, "bq_c"), (bk_c, "bk_c"),
                          (wg_sb, "wg")):
            sy.dma_start(dst[:, :], ap[srcn][:, :]).then_inc(s_pre, 16)
            npre += 16



        Ws(se, "s", s_pre, "s_pre", npre)
        Ws(ve, "v", s_pre, "s_pre", npre)
        Ws(te, "t", s_pre, "s_pre", npre)
        tok = cs.inc(se.activation(wgsq[:, :], wg_sb[:, :], AF.Square, accum_out=nrm2[:, :]))
        W(se, "s", cs, tok)
        tok = cs.inc(se.activation(nrm[:, :], nrm2[:, :], AF.Sqrt))
        W(ve, "v", cs, tok)
        tok = cv.inc(ve.tensor_scalar_max(nrm[:, :], nrm[:, :], 1e-4))
        W(ve, "v", cv, tok)
        tok = cv.inc(ve.reciprocal(rinv[:, :], nrm[:, :]))
        W(ve, "v", cv, tok)
        tok_wgu = cv.inc(ve.tensor_scalar_mul(wgu[:, :], wg_sb[:, :], rinv[:, :]))
        tok_bqs = cv.inc(ve.tensor_scalar_mul(bqs[:, :], bq_c[:, :], SCALE))
        tok_bks = cv.inc(ve.tensor_scalar_mul(bks[:, :], bk_c[:, :], SCALE))
        for g in range(NG):
            cv.inc(ve.memset(state[g][:, :], 0.0))
        tok = cv.inc(ve.memset(c_eps[:, :], 1e-5))
        tok_c1 = cv.inc(ve.memset(c_one[:, :], 1.0))
        W(se, "s", cv, tok_c1)
        W(te, "t", cv, tok_wgu)
        pe_begin(R["P3a"])
        pe_end(te.transpose(P3a[0:16, 0:8], wgu[:, :], ident32[0:8, 0:8]), R["P3a"])
        rd_begin(ve, "v", R["P3a"])
        tok = cv.inc(ve.tensor_copy(wguT[:, :], P3a[0:16, 0:8]))
        rd_end(cv, tok, R["P3a"])
        for half_c in range(2):
            W(sy, "sy", ct, ct.v)
            sy.dma_start(wgred_sb[:, :],
                         ap["wgred"][:, half_c * 512:(half_c + 1) * 512]).then_inc(s_pre, 16)
            npre += 16
            Ws(te, "t", s_pre, "s_pre", npre)
            W(te, "t", cv, tok)
            pe_begin(R["P3a"])
            pe_end(te.matmul(P3a[0:8, 0:512], wguT[:, :], wgred_sb[:, :],
                             start=True, stop=True), R["P3a"])
            rd_begin(ve, "v", R["P3a"])
            tok = cv.inc(ve.tensor_copy(wred8T[:, :], P3a[0:8, 0:512]))
            rd_end(cv, tok, R["P3a"])
            for ic4 in range(4):
                ic = half_c * 4 + ic4
                W(te, "t", cv, tok)
                pe_begin(R["P3a"])
                pe_end(te.transpose(P3a[:, 0:8], wred8T[:, ic4 * 128:(ic4 + 1) * 128],
                                    ident32[0:8, 0:8]), R["P3a"])
                rd_begin(ve, "v", R["P3a"])
                tok = cv.inc(ve.tensor_copy(wred8[:, ic * 8:(ic + 1) * 8], P3a[:, 0:8]))
                rd_end(cv, tok, R["P3a"])
        st_ov.close()

        SC_T = 512
        NSC = HALF // SC_T

        def attn_phase(half):
            hb = half * HALF
            slab_rel = [0, 0]
            adT_rel = [0, 0]
            mem_tok = [0]
            exp_rel = [0]
            admul_rel = [0]
            curve_tok = [0, 0]
            ct2_tok = [0, 0]
            proj_cs = [0, 0]; proj_cv = [0, 0]
            sbf_tok = [0]
            for grp in range(NG):
                gofs = grp * GW
                if wa_rel[0] is not None:
                    W(gp, "gp", wa_rel[0][0], wa_rel[0][1])
                for n, wn in (("wq", "wqT"), ("wk", "wkT"), ("wv", "wvT"),
                              ("wb", "wbT"), ("wa", "waT")):
                    gp.dma_start(
                        slab[n][:, :].rearrange("p (i o) -> p i o", o=GW),
                        ap[wn][:, gofs:gofs + GW].rearrange("(i p) o -> p i o", p=128)
                    ).then_inc(s_wa, 16)
                    wa_cnt[0] += 1
                for i, bn in enumerate(("bv_r", "bb_r", "ba_r", "bk_r")):
                    gp.dma_start(brow4[i][:, :], ap[bn][0:1, gofs:gofs + GW]).then_inc(s_wa, 16)
                    wa_cnt[0] += 1
                gp.dma_start(browq[:, :], ap["bq_r"][0:1, gofs:gofs + GW]).then_inc(s_wa, 16)
                gp.dma_start(browk[:, :], ap["bk_r"][0:1, gofs:gofs + GW]).then_inc(s_wa, 16)
                wa_cnt[0] += 2
                Ws(te, "t", s_wa, "s_wa", 16 * wa_cnt[0])

                pseq = [0]
                W(ve, "v", ct, mem_tok[0])
                W(ve, "v", cv, cv.v)
                sbf_tok[0] = cv.inc(ve.tensor_copy(state_bf[:, :], state[grp][:, :]))

                def proj_gen(sc):
                    p = sc % 2
                    W(se, "s", ct, slab_rel[p])
                    W(ve, "v", ct, slab_rel[p])
                    for wname, dstq, brq in (("wq", q_sl[p], browq), ("wk", k_sl[p], browk)):
                        for oc in range(4):
                            reg = R["P0a"] if pseq[0] % 2 == 0 else R["P0b"]
                            cof = 0 if pseq[0] % 2 == 0 else 512
                            pseq[0] += 1
                            pe_begin(reg)
                            for ic in range(IC):
                                te.matmul(
                                    P0[:, cof:cof + 512],
                                    slab[wname][:, ic * GW + oc * 128: ic * GW + (oc + 1) * 128],
                                    xnT[:, ic * HALF + sc * SC_T: ic * HALF + (sc + 1) * SC_T],
                                    start=(ic == 0), stop=False)
                            last = te.matmul(P0[:, cof:cof + 512],
                                             brq[0:1, oc * 128:(oc + 1) * 128],
                                             ones_bf[0:1, 0:512], start=False, stop=True)
                            pe_end(last, reg)
                            rd_begin(se, "s", reg)
                            W(se, "s", cv, cv.v)
                            tok = cs.inc(se.mul(dstq[:, oc * 512:(oc + 1) * 512],
                                                P0[:, cof:cof + 512], SCALE))
                            rd_end(cs, tok, reg)
                            yield
                    for tc2 in range(4):
                        for wi, wname in enumerate(("wv", "wb", "wa", "wk")):
                            reg = R["P0a"] if pseq[0] % 2 == 0 else R["P0b"]
                            cof = 0 if pseq[0] % 2 == 0 else 512
                            pseq[0] += 1
                            pe_begin(reg)
                            for ic in range(IC):
                                te.matmul(
                                    P0[:, cof:cof + 512],
                                    xnT[:, ic * HALF + sc * SC_T + tc2 * 128:
                                        ic * HALF + sc * SC_T + (tc2 + 1) * 128],
                                    slab[wname][:, ic * GW:(ic + 1) * GW],
                                    start=(ic == 0), stop=False)
                            brow_i = brow4[3] if wi == 3 else brow4[wi]
                            last = te.matmul(P0[:, cof:cof + 512], ones_bf[0:1, 0:128],
                                             brow_i[:, :], start=False, stop=True)
                            pe_end(last, reg)
                            if wi == 0:
                                rd_begin(se, "s", reg)
                                W(se, "s", cv, cv.v)
                                tok_v = cs.inc(se.copy(v_sl[:, :],
                                                       P0[:, cof:cof + 512]))
                                rd_end(cs, tok_v, reg)
                            elif wi == 3:
                                rd_begin(se, "s", reg)
                                W(se, "s", cv, cv.v)
                                tok = cs.inc(se.mul(ktok[p][:, tc2 * GW:(tc2 + 1) * GW],
                                                    P0[:, cof:cof + 512], SCALE))
                                rd_end(cs, tok, reg)
                            elif wi == 1:
                                rd_begin(se, "s", reg)
                                W(se, "s", cv, cv.v)
                                tok_l1 = cs.inc(se.copy(L1[:, :], P0[:, cof:cof + 512]))
                                rd_end(cs, tok_l1, reg)
                                W(ve, "v", cs, tok_l1)
                                tok = cv.inc(ve.scalar_tensor_tensor(
                                    M2[:, :], L1[:, :], -1.0, L1[:, :], OP.mult, OP.min))
                                W(se, "s", cv, tok)
                                tok = cs.inc(se.activation(M2[:, :], M2[:, :], AF.Exp))
                                W(se, "s", cs, tok)
                                tok_ln = cs.inc(se.activation(M2[:, :], M2[:, :], AF.Ln,
                                                              bias=c_one[:, :]))
                                W(ve, "v", cs, tok_ln)
                                for h in range(8):
                                    tok_sp = cv.inc(ve.scalar_tensor_tensor(
                                        BSP[:, h * 64:(h + 1) * 64], L1[:, h * 64:(h + 1) * 64],
                                        0.0, M2[:, h * 64:(h + 1) * 64], OP.max, OP.add,
                                        accum_out=bmsum[:, h:h + 1]))
                                W(ve, "v", cv, tok_sp)
                                tok_vb = cv.inc(ve.tensor_mul(
                                    vb_sl[p][:, tc2 * GW:(tc2 + 1) * GW],
                                    v_sl[:, :], BSP[:, :]))
                                vview = v_sl[:, :].rearrange(
                                    "p (h d) -> p h d", d=64)
                                mview = vbm_sl[p][:, tc2 * GW:(tc2 + 1) * GW].rearrange(
                                    "p (h d) -> p h d", d=64)
                                bmv = bmsum[:, :].rearrange("p (h d) -> p h d", d=1)
                                b0, b1 = bass.broadcast_tensor_aps(vview, bmv)
                                tok = cv.inc(ve.scalar_tensor_tensor(
                                    mview, b0, 1.0 / 64, b1, OP.mult, OP.mult))
                            else:
                                rd_begin(se, "s", reg)
                                W(se, "s", cv, cv.v)
                                tok_l1 = cs.inc(se.copy(L1[:, :], P0[:, cof:cof + 512]))
                                rd_end(cs, tok_l1, reg)
                                W(ve, "v", cs, tok_l1)
                                tok = cv.inc(ve.tensor_scalar(CSb[:, :], L1[:, :], 10.0, -10.0,
                                                              OP.min, OP.max))
                                W(ve, "v", cv, tok)
                                tok = cv.inc(ve.tensor_scalar_mul(NEG[:, :], CSb[:, :], -1.0))
                                W(ve, "v", cv, tok)
                                tok = cv.inc(ve.scalar_tensor_tensor(
                                    M2[:, :], CSb[:, :], -1.0, CSb[:, :], OP.mult, OP.min))
                                W(se, "s", cv, tok)
                                tok = cs.inc(se.activation(M2[:, :], M2[:, :], AF.Exp))
                                W(se, "s", cs, tok)
                                tok_ln = cs.inc(se.activation(M2[:, :], M2[:, :], AF.Ln,
                                                              bias=c_one[:, :]))
                                W(ve, "v", cs, tok_ln)
                                for h in range(8):
                                    cv.inc(ve.scalar_tensor_tensor(
                                        JUNK[:, h * 64:(h + 1) * 64], NEG[:, h * 64:(h + 1) * 64],
                                        0.0, M2[:, h * 64:(h + 1) * 64], OP.max, OP.add,
                                        accum_out=lasum[p][:, tc2 * 8 + h: tc2 * 8 + h + 1]))
                            yield
                    proj_cs[p] = cs.v
                    proj_cv[p] = cv.v

                def smalls(sc):
                    p = sc % 2
                    W(te, "t", cv, proj_cv[p])
                    W(te, "t", cs, proj_cs[p])
                    pe_begin(R["P8"])
                    last = None
                    for c2 in range(4):
                        last = te.matmul(P8[:, 384 + c2 * 8:392 + c2 * 8], u64[:, :],
                                         lasum[p][:, c2 * 8:(c2 + 1) * 8],
                                         start=True, stop=True)
                    pe_end(last, R["P8"])
                    pe_begin(R["P1a"], R["P1b"])
                    last = None
                    for c2 in range(4):
                        last = te.matmul(P1[0:8, c2 * 128:(c2 + 1) * 128],
                                         lasum[p][:, c2 * 8:(c2 + 1) * 8],
                                         u64[:, :], start=True, stop=True)
                    pe_end(last, R["P1a"], R["P1b"])
                    rd_begin(ve, "v", R["P8"], R["P1a"], R["P1b"])
                    tok = cv.inc(ve.tensor_copy(cums_b[p][:, :], P8[:, 384:416]))
                    tok_ct2 = cv.inc(ve.tensor_copy(cumsT_b[p][:, :], P1[0:8, 0:512]))
                    rd_end(cv, tok_ct2, R["P8"], R["P1a"], R["P1b"])
                    ct2_tok[p] = tok_ct2
                    W(se, "s", cv, tok_ct2)
                    W(se, "s", cv, cv.v)
                    curve_tok[p] = cs.inc(se.activation(
                        curve_b[p][:, :], cums_b[p][:, :], AF.Exp, scale=-1.0))
                    W(te, "t", cv, tok_ct2)
                    pe_begin(R["P8"])
                    last = None
                    for c2 in range(4):
                        last = te.matmul(P8[0:1, 416 + c2 * 8:424 + c2 * 8],
                                         cumsT_b[p][:, c2 * 128 + 127: c2 * 128 + 128],
                                         ident32[0:8, 0:8], start=True, stop=True)
                    pe_end(last, R["P8"])
                    rd_begin(se, "s", R["P8"])
                    tok_as = cs.inc(se.activation(JUNK[0:1, 0:32], P8[0:1, 416:448],
                                                  AF.Exp, scale=-1.0))
                    rd_end(cs, tok_as, R["P8"])
                    W(te, "t", cs, tok_as)
                    pe_begin(R["P8"])
                    last = None
                    for c2 in range(4):
                        last = te.matmul(P8[:, 448 + c2 * 8:456 + c2 * 8], ones_r[:, :],
                                         JUNK[0:1, c2 * 8:(c2 + 1) * 8],
                                         start=True, stop=True)
                    pe_end(last, R["P8"])
                    rd_begin(ve, "v", R["P8"])
                    tok = cv.inc(ve.tensor_copy(asumb_b[p][:, :], P8[:, 448:480]))
                    rd_end(cv, tok, R["P8"])

                def prep(k):
                    sc, c2 = divmod(k, 4)
                    p = sc % 2
                    W(te, "t", cv, ct2_tok[p])
                    W(te, "t", cs, proj_cs[p])
                    W(te, "t", cv, proj_cv[p])
                    pe_begin(R["P1a"], R["P1b"])
                    last = None
                    for h in range(8):
                        last = te.matmul(P1[:, h * 128:(h + 1) * 128],
                                         onehot8[:, h * 128:(h + 1) * 128],
                                         cumsT_b[p][:, c2 * 128:(c2 + 1) * 128],
                                         start=True, stop=True)
                    pe_end(last, R["P1a"], R["P1b"])
                    pe_begin(R["P2a"], R["P2b"])
                    last = None
                    for h in range(8):
                        oc, poff = h // 2, (h % 2) * 64
                        dst = P2[:, (h // 2) * 128:(h // 2 + 1) * 128] if h % 2 == 0 \
                            else P2[:, 512 + (h // 2) * 128: 512 + (h // 2 + 1) * 128]
                        last = te.matmul(
                            dst,
                            k_sl[p][poff:poff + 64, oc * 512 + c2 * 128: oc * 512 + (c2 + 1) * 128],
                            q_sl[p][poff:poff + 64, oc * 512 + c2 * 128: oc * 512 + (c2 + 1) * 128],
                            start=True, stop=True)
                    pe_end(last, R["P2a"], R["P2b"])
                    rd_begin(ve, "v", R["P1a"], R["P1b"])
                    bview = P1[:, :].rearrange("p (h i) -> p h i", i=128)
                    cview = cums_b[p][:, c2 * 8:(c2 + 1) * 8].rearrange("p (h i) -> p h i", i=1)
                    b0, b1 = bass.broadcast_tensor_aps(bview, cview)
                    tok = cv.inc(ve.tensor_tensor(
                        Dsb[:, :].rearrange("p (h i) -> p h i", i=128), b0, b1, OP.subtract))
                    rd_end(cv, tok, R["P1a"], R["P1b"])
                    W(ve, "v", cv, tok)
                    W(ve, "v", cs, exp_rel[0])
                    tok_dm = cv.inc(ve.select(Dm_all[:, :], triu8[:, :], Dsb[:, :],
                                              posbig[:, :], add_drain=True))
                    W(se, "s", cv, tok_dm)
                    W(se, "s", cv, admul_rel[0])
                    tok_dc = cs.inc(se.activation(dec_all[:, :], Dm_all[:, :], AF.Exp,
                                                  scale=-1.0))
                    exp_rel[0] = tok_dc
                    W(ve, "v", cs, tok_dc)
                    W(ve, "v", ct, adT_rel[k % 2])
                    rd_begin(ve, "v", R["P2a"], R["P2b"])
                    tok_ad = None
                    for par2 in range(2):
                        srcp = P2[:, 0:512] if par2 == 0 else P2[:, 512:1024]
                        aview = srcp.rearrange("p (a o i) -> p a o i", o=1, i=128)
                        dview = dec_all[:, :].rearrange("p (a b i) -> p a b i", b=2, i=128)[
                            :, :, par2:par2 + 1, :]
                        oview = adT[k % 2][:, :].rearrange("p (a b i) -> p a b i", b=2, i=128)[
                            :, :, par2:par2 + 1, :]
                        tok_ad = cv.inc(ve.tensor_tensor(oview, aview, dview, OP.mult))
                    admul_rel[0] = tok_ad
                    rd_end(cv, tok_ad, R["P2a"], R["P2b"])

                def scan(k):
                    sc, c2 = divmod(k, 4)
                    p = sc % 2
                    gi = k
                    W(te, "t", cv, sbf_tok[0])
                    pe_begin(R["P3a"])
                    last = None
                    for h in range(8):
                        oc = h // 2
                        last = te.matmul(
                            P3a[:, h * 64:(h + 1) * 64],
                            q_sl[p][:, oc * 512 + c2 * 128: oc * 512 + (c2 + 1) * 128],
                            state_bf[:, h * 64:(h + 1) * 64], start=True, stop=True)
                    tok_mm = pe_end(last, R["P3a"])
                    mem_tok[0] = tok_mm
                    W(te, "t", cv, admul_rel[0])
                    pe_begin(R["P0a"])
                    last = None
                    for h in range(8):
                        last = te.matmul(
                            P0[:, h * 64:(h + 1) * 64],
                            adT[k % 2][:, h * 128:(h + 1) * 128],
                            vb_sl[p][:, c2 * GW + h * 64: c2 * GW + (h + 1) * 64],
                            start=True, stop=True)
                    tok_yi_mm = pe_end(last, R["P0a"])
                    adT_rel[k % 2] = tok_yi_mm
                    pe_begin(R["P8"])
                    last = None
                    for h in range(8):
                        poff = (h % 2) * 64
                        last = te.matmul(
                            P8[poff:poff + 64, (h // 2) * 64:(h // 2 + 1) * 64],
                            ktok[p][:, c2 * GW + h * 64: c2 * GW + (h + 1) * 64],
                            vbm_sl[p][:, c2 * GW + h * 64: c2 * GW + (h + 1) * 64],
                            start=True, stop=True)
                    tok_dl = pe_end(last, R["P8"])
                    slab_rel[p] = tok_dl
                    rd_begin(ve, "v", R["P8"])
                    W(ve, "v", ct, mem_tok[0])
                    W(ve, "v", cv, cv.v)
                    tok_su = None
                    for h in range(8):
                        poff = (h % 2) * 64
                        cv.inc(ve.scalar_tensor_tensor(
                            state_bf[poff:poff + 64, h * 64:(h + 1) * 64],
                            state[grp][poff:poff + 64, h * 64:(h + 1) * 64],
                            asumb_b[p][poff:poff + 64, c2 * 8 + h: c2 * 8 + h + 1],
                            P8[poff:poff + 64, (h // 2) * 64:(h // 2 + 1) * 64],
                            OP.mult, OP.add))
                    for h in range(8):
                        poff = (h % 2) * 64
                        tok_su = cv.inc(ve.scalar_tensor_tensor(
                            state[grp][poff:poff + 64, h * 64:(h + 1) * 64],
                            state[grp][poff:poff + 64, h * 64:(h + 1) * 64],
                            asumb_b[p][poff:poff + 64, c2 * 8 + h: c2 * 8 + h + 1],
                            P8[poff:poff + 64, (h // 2) * 64:(h // 2 + 1) * 64],
                            OP.mult, OP.add))
                    sbf_tok[0] = tok_su
                    rd_end(cv, tok_su, R["P8"])
                    rd_begin(se, "s", R["P0a"])
                    W(se, "s", cv, cv.v)
                    tok_yi = cs.inc(se.copy(yi_sb[:, :], P0[:, 0:512]))
                    rd_end(cs, tok_yi, R["P0a"])
                    W(ve, "v", cs, tok_yi)
                    W(ve, "v", cs, curve_tok[p])
                    rd_begin(ve, "v", R["P3a"])
                    mview = P3a[:, :].rearrange("p (h d) -> p h d", d=64)
                    cuv = curve_b[p][:, c2 * 8:(c2 + 1) * 8].rearrange("p (h d) -> p h d", d=1)
                    m0, m1 = bass.broadcast_tensor_aps(mview, cuv)
                    tok = cv.inc(ve.tensor_tensor(
                        tmc[:, :].rearrange("p (h d) -> p h d", d=64), m0, m1, OP.mult))
                    W(ve, "v", cv, tok)
                    Ws(ve, "v", s_t[c2 % 2], f"s_t{c2 % 2}", 16 * t_cnt[c2 % 2])
                    tok_ep = cv.inc(ve.tensor_add(ybf[c2 % 2][:, :], tmc[:, :], yi_sb[:, :]))
                    rd_end(cv, tok_ep, R["P3a"])
                    W(sy, "sy", cv, tok_ep)
                    sy.dma_start(y_tok[hb + gi * 128: hb + (gi + 1) * 128, gofs:gofs + GW],
                                 ybf[c2 % 2][:, :]).then_inc(s_t[c2 % 2], 16)
                    t_cnt[c2 % 2] += 1

                for _ in proj_gen(0):
                    pass
                smalls(0)
                for sc in range(NSC):
                    pg = proj_gen(sc + 1) if sc < NSC - 1 else iter(())

                    def pump(n):
                        for _ in range(n):
                            if next(pg, "D") == "D":
                                return

                    prep(4 * sc)
                    pump(3)
                    prep(4 * sc + 1)
                    pump(3)
                    scan(4 * sc)
                    pump(3)
                    prep(4 * sc + 2)
                    pump(3)
                    scan(4 * sc + 1)
                    pump(3)
                    prep(4 * sc + 3)
                    pump(3)
                    scan(4 * sc + 2)
                    pump(24)
                    if sc < NSC - 1:
                        smalls(sc + 1)
                    scan(4 * sc + 3)
                wa_rel[0] = (ct, ct.v)

        for half in range(2):
            hb = half * HALF
            barrier()
            x_load(0, hb)
            x_load(1, hb + 128)
            ln_tr_tok = [0, 0]
            ln_xnt_tok = [0, 0]
            aggr_tok = [0, 0]

            def ln_stats(tcx):
                parx = tcx % 2
                x_wait(ve, "v", parx)
                cv.inc(ve.bn_stats(bnst[parx][:, 0:6], x_sb[parx][:, 0:512]))
                tok2 = cv.inc(ve.bn_stats(bnst[parx][:, 6:12], x_sb[parx][:, 512:1024]))
                W(ve, "v", cv, tok2)
                aggr_tok[parx] = cv.inc(ve.bn_aggr(mv[parx][:, :], bnst[parx][:, :]))

            ln_stats(0)
            for tc in range(NCH):
                par = tc % 2
                gc = half * NCH + tc
                xn_t = out_sb[par]
                x32 = Dsb if par == 0 else Dm_all
                W(se, "s", cv, aggr_tok[par])
                tok_s = cs.inc(se.activation(stdv[par][:, :], mv[par][:, 1:2], AF.Sqrt,
                                             bias=c_eps[:, :]))
                if tc + 1 < NCH:
                    ln_stats(tc + 1)
                W(ve, "v", cs, tok_s)
                tok = cv.inc(ve.reciprocal(rstd[par][:, :], stdv[par][:, :]))
                W(ve, "v", cv, tok)
                W(ve, "v", ct, ln_tr_tok[par])
                if o_rel[par] is not None:
                    W(ve, "v", o_rel[par][0], o_rel[par][1])
                Ws(ve, "v", s_o[par], f"s_o{par}", 16 * o_cnt[par])
                tok_xn = cv.inc(ve.tensor_scalar(xn_t[:, :], x_sb[par][:, :],
                                                 mv[par][:, 0:1], rstd[par][:, :],
                                                 OP.subtract, OP.mult))
                x_rel[par] = (cv, tok_xn)
                if tc + 2 < NCH:
                    x_load(par, hb + (tc + 2) * 128)
                W(te, "t", cv, tok_xn)
                regs_ = (R["P0a"], R["P0b"]) if par == 0 else (R["P2a"], R["P2b"])
                PT = P0 if par == 0 else P2
                pe_begin(*regs_)
                last = None
                for ic in range(IC):
                    last = te.transpose(PT[:, ic * 128:(ic + 1) * 128],
                                        xn_t[:, ic * 128:(ic + 1) * 128], ident32[:, :])
                pe_end(last, *regs_)
                rd_begin(ve, "v", *regs_)
                W(ve, "v", cv, ln_xnt_tok[par])
                tok = cv.inc(ve.tensor_copy(x32[:, 0:512], PT[:, 0:512]))
                tok32 = cv.inc(ve.tensor_copy(x32[:, 512:1024], PT[:, 512:1024]))
                rd_end(cv, tok32, *regs_)
                W(ve, "v", cv, tok32)
                tok_xnt = cv.inc(ve.tensor_copy(
                    xnT[:, :].rearrange("p (i t) -> p i t", t=HALF)[:, :, tc * 128:(tc + 1) * 128],
                    x32[:, :].rearrange("p (i t) -> p i t", t=128)))
                Ws(ve, "v", s_t[par], f"s_t{par}", 16 * t_cnt[par])
                tok_xb = cv.inc(ve.tensor_copy(xnbf[par][:, :], xn_t[:, :]))
                W(sy, "sy", cv, tok_xb)
                sy.dma_start(xn_tok[hb + tc * 128: hb + (tc + 1) * 128, :],
                             xnbf[par][:, :]).then_inc(s_t[par], 16)
                t_cnt[par] += 1
                W(te, "t", cv, tok32)
                greg = R["P3a"] if par == 0 else R["P8"]
                GP = P3a if par == 0 else P8
                pe_begin(greg)
                last = None
                for ic in range(IC):
                    last = te.matmul(GP[:, 0:8], x32[:, ic * 128:(ic + 1) * 128],
                                     wred8[:, ic * 8:(ic + 1) * 8],
                                     start=(ic == 0), stop=(ic == IC - 1))
                tok_g = pe_end(last, greg)
                ln_tr_tok[par] = tok_g
                rd_begin(ve, "v", greg)
                tok = cv.inc(ve.tensor_copy(lg[par][:, :], GP[:, 0:8]))
                rd_end(cv, tok, greg)
                ln_xnt_tok[par] = tok if False else ln_xnt_tok[par]
                W(ve, "v", cv, tok)
                tok = cv.inc(ve.reduce_max(ngmx[par][:, :], lg[par][:, :], axis=AX.X,
                                           negate=True))
                W(se, "s", cv, tok)
                tok_e = cs.inc(se.activation(expd[par][:, :], lg[par][:, :], AF.Exp,
                                             bias=ngmx[par][:, :]))
                W(ve, "v", cs, tok_e)
                tok = cv.inc(ve.reduce_sum(ssum[par][:, :], expd[par][:, :], axis=AX.X))
                W(ve, "v", cv, tok)
                tok = cv.inc(ve.reciprocal(score_sl[:, gc:gc + 1], ssum[par][:, :]))
                tok = cv.inc(ve.max(mx8[par][:, :], lg[par][:, :]))
                W(ve, "v", cv, tok)
                tok = cv.inc(ve.max_index(ix8[par][:, :], mx8[par][:, :], lg[par][:, :]))
                W(ve, "v", cv, tok)
                tok = cv.inc(ve.tensor_copy(idx_sl[:, gc:gc + 1], ix8[par][:, 0:1]))
                W(ve, "v", cv, tok)
                for e in range(E):
                    tok_m = cv.inc(ve.tensor_single_scalar(
                        msk_sl[:, gc * E + e:gc * E + e + 1], idx_sl[:, gc:gc + 1],
                        float(e), OP.is_equal))
                ln_xnt_tok[par] = tok_m

            attn_phase(half)

        barrier()
        ov_a.close()
        ov_b = ExitStack()
        bsb = lambda name, shape, dt=F32: ov_b.enter_context(nc.sbuf_tensor(name, list(shape), dt))
        w1e = [bsb(f"w1e{i}", (128, IC * 512), BF16) for i in range(3)]
        w2e = [bsb(f"w2e{i}", (128, HCE * 1024), BF16) for i in range(3)]
        h_sb = [bsb(f"h_sb{i}", (128, HCE * 512), BF16) for i in range(2)]
        b1c = bsb("p_b1c", (128, 256))
        b2all = bsb("p_b2all", (1, E * D), BF16)
        xnT_pe = [bsb(f"xnT_pe{i}", (128, IC * CAP), BF16) for i in range(2)]
        ffn_acc = bsb("ffn_acc", (128, NST * D), BF16)
        xperm_t = [bsb(f"xperm_t{i}", (128, D), BF16) for i in range(4)]
        ybuf = [bsb(f"ybuf{i}", (128, D), BF16) for i in range(2)]
        fbuf = [bsb(f"fbuf{i}", (128, D), BF16) for i in range(2)]
        barrier()
        sy.dma_start(b1c[:, :], ap["b1_c"][:, :]).then_inc(s_pre, 16)
        npre += 16
        Ws(se, "s", s_pre, "s_pre", npre)

        for c in range(2 * NCH):
            pe_begin(R["P8"])
            pe_end(te.matmul(P8[0:8, 0:1], msk_sl[:, c * E:(c + 1) * E],
                             onescol_bf[:, :], start=True, stop=True), R["P8"])
            rd_begin(ve, "v", R["P8"])
            tok = cv.inc(ve.tensor_copy(tot_sb[:, c:c + 1], P8[0:8, 0:1]))
            rd_end(cv, tok, R["P8"])
        W(te, "t", cv, cv.v)
        pe_begin(R["P8"])
        pe_end(te.transpose(P8[0:32, 8:16], tot_sb[:, :], ident32[0:8, 0:8]), R["P8"])
        rd_begin(ve, "v", R["P8"])
        tok = cv.inc(ve.tensor_copy(asb[0:32, 0:8], P8[0:32, 8:16]))
        rd_end(cv, tok, R["P8"])
        W(te, "t", cv, tok)
        pe_begin(R["P8"])
        pe_end(te.matmul(P8[0:32, 16:24], u1x32[:, :], asb[0:32, 0:8],
                         start=True, stop=True), R["P8"])
        rd_begin(ve, "v", R["P8"])
        tok = cv.inc(ve.tensor_copy(cb_sb[:, :], P8[0:32, 16:24]))
        rd_end(cv, tok, R["P8"])
        pos_tok = 0
        for c in range(2 * NCH):
            W(te, "t", cv, cv.v)
            pe_begin(R["P8"])
            te.matmul(P8[:, 24:32], u1bf[:, :], msk_sl[:, c * E:(c + 1) * E],
                      start=True, stop=True)
            pe_end(te.matmul(P8[0:1, 32:40], ident32[0:32, c:c + 1], cb_sb[:, :],
                             start=True, stop=True), R["P8"])
            rd_begin(ve, "v", R["P8"])
            tok = cv.inc(ve.tensor_copy(asb[:, :], P8[:, 24:32]))
            tok = cv.inc(ve.tensor_copy(rowsb[:, :], P8[0:1, 32:40]))
            rd_end(cv, tok, R["P8"])
            W(ve, "v", cv, tok)
            tok_row = cv.inc(ve.tensor_add(rowsb[:, :], rowsb[:, :], gbase_r[:, :]))
            W(te, "t", cv, tok_row)
            pe_begin(R["P8"])
            pe_end(te.matmul(P8[:, 40:48], ones_r[:, :], rowsb[:, :],
                             start=True, stop=True), R["P8"])
            W(ve, "v", cv, tok_row)
            tok = cv.inc(ve.tensor_tensor(t3sb[:, :], msk_sl[:, c * E:(c + 1) * E],
                                          asb[:, :], OP.mult))
            W(ve, "v", cv, tok)
            tok = cv.inc(ve.reduce_sum(rk_sb[:, :], t3sb[:, :], axis=AX.X))
            W(ve, "v", cv, tok)
            tok = cv.inc(ve.tensor_scalar(ov_sb[:, :], rk_sb[:, :], float(CAP), 1e9,
                                          OP.is_gt, OP.mult))
            rd_begin(ve, "v", R["P8"])
            tok = cv.inc(ve.tensor_add(absb[:, :], asb[:, :], P8[:, 40:48]))
            rd_end(cv, tok, R["P8"])
            W(ve, "v", cv, tok)
            tok = cv.inc(ve.tensor_tensor(t3sb[:, :], msk_sl[:, c * E:(c + 1) * E],
                                          absb[:, :], OP.mult))
            W(ve, "v", cv, tok)
            tok = cv.inc(ve.reduce_sum(posf[:, :], t3sb[:, :], axis=AX.X))
            W(ve, "v", cv, tok)
            tok = cv.inc(ve.scalar_tensor_tensor(posf[:, :], posf[:, :], -1.0,
                                                 ov_sb[:, :], OP.add, OP.add))
            W(ve, "v", cv, tok)
            pos_tok = cv.inc(ve.tensor_copy(pos_sl[:, c:c + 1], posf[:, :]))
        tok_z = cv.inc(ve.memset(ffn_acc[:, :], 0.0))
        W(sy, "sy", cv, tok_z)
        for e8 in range(E):
            sy.dma_start(
                xn_perm[e8 * CAP:(e8 + 1) * CAP, :].rearrange("(t p) c -> p t c", p=128),
                ffn_acc[:, :].rearrange("p (t c) -> p t c", c=D)).then_inc(s_t[e8 % 2], 16)
            t_cnt[e8 % 2] += 1
        for p_ in range(2):
            Ws(gp, "gp", s_t[p_], f"s_t{p_}", 16 * t_cnt[p_])
        W(gp, "gp", cv, pos_tok)
        for c in range(2 * NCH):
            par = c % 2
            Ws(sy, "sy", s_g[par], f"s_g{par}", 16 * g_cnt[par])
            sy.dma_start(xperm_t[par][:, :],
                         xn_tok[c * 128:(c + 1) * 128, :]).then_inc(s_r[par], 16)
            r_cnt[par] += 1
            Ws(gp, "gp", s_r[par], f"s_r{par}", 16 * r_cnt[par])
            gp.indirect_dma_start(
                xn_perm[:, :], bass.IndirectOffsetOnAxis(pos_sl[:, c:c + 1], 0),
                xperm_t[par][:, :], None,
                bounds_check=bc_reg, oob_is_err=False).then_inc(s_g[par], 16)
            g_cnt[par] += 1

        def w_load(it):
            e_, we_ = divmod(it, NWE)
            par = it % 3
            if w_rel[par] is not None:
                W(gp, "gp", w_rel[par][0], w_rel[par][1])
            gp.dma_start(
                w1e[par][:, :].rearrange("p (i h) -> p i h", h=512),
                ap["w1"][e_, :, we_ * 512:(we_ + 1) * 512].rearrange(
                    "(i p) h -> p i h", p=128)).then_inc(s_w[par], 16)
            gp.dma_start(
                w2e[par][:, :].rearrange("p (i h) -> p i h", h=1024),
                ap["w2"][e_, we_ * 512:(we_ + 1) * 512, :].rearrange(
                    "(i p) h -> p i h", p=128)).then_inc(s_w[par], 16)
            w_cnt[par] += 2

        tr_units = []
        xp_rel = [0, 0, 0, 0]
        xnTpe_rel = [0, 0]
        load_cnt = {}

        def emit_tile_load(e_, st):
            par = (e_ * NST + st) % 4
            pv = xp_rel[par]
            if pv:
                W(sy, "sy", ct, pv)
            Ws(sy, "sy", s_g[0], "s_g0", 16 * g_cnt[0])
            Ws(sy, "sy", s_g[1], "s_g1", 16 * g_cnt[1])
            sy.dma_start(xperm_t[par][:, :],
                         xn_perm[e_ * CAP + st * 128: e_ * CAP + (st + 1) * 128, :]
                         ).then_inc(s_r[par], 16)
            r_cnt[par] += 1
            load_cnt[(e_, st)] = r_cnt[par]

        def make_tr_unit(e_, st, halfu):
            par = (e_ * NST + st) % 4
            pe = e_ % 2

            def emit():
                Ws(te, "t", s_r[par], f"s_r{par}", 16 * load_cnt[(e_, st)])
                pe_begin(R["P3a"])
                last = None
                for j in range(4):
                    ic = halfu * 4 + j
                    last = te.matmul(P3a[:, j * 128:(j + 1) * 128],
                                     xperm_t[par][:, ic * 128:(ic + 1) * 128],
                                     identbf[:, :], start=True, stop=True)
                tok_t = pe_end(last, R["P3a"])
                if halfu == 1:
                    xp_rel[par] = tok_t
                rd_begin(ve, "v", R["P3a"])
                W(ve, "v", ct, xnTpe_rel[pe])
                dst = xnT_pe[pe][:, :].rearrange("p (i t) -> p i t", t=CAP)
                tok = cv.inc(ve.tensor_copy(
                    dst[:, halfu * 4:(halfu + 1) * 4, st * 128:(st + 1) * 128],
                    P3a[:, :].rearrange("p (j t) -> p j t", t=128)))
                rd_end(cv, tok, R["P3a"])
                if halfu == 1 and st + 4 < NST:
                    emit_tile_load(e_, st + 4)
            return emit

        def queue_expert_transposes(e_):
            for st0 in range(4):
                emit_tile_load(e_, st0)
            for st in range(NST):
                for halfu in range(2):
                    tr_units.append(make_tr_unit(e_, st, halfu))

        def pump_tr(n=1):
            for _ in range(n):
                if tr_units:
                    tr_units.pop(0)()

        barrier(("gp",))
        gp.dma_start(b2all[:, :], ap["b2r"][:, :]).then_inc(s_g[0], 16)
        g_cnt[0] += 1
        w_load(0)
        w_load(1)
        queue_expert_transposes(0)
        pump_tr(len(tr_units))
        h_rel = [None, None]
        ffn_w_tok = [0, 0]
        for it in range(E * NWE):
            e_, we_ = divmod(it, NWE)
            par = it % 3
            pe = e_ % 2
            if it + 2 < E * NWE:
                w_load(it + 2)
            if we_ == 0:
                if e_ + 1 < E:
                    queue_expert_transposes(e_ + 1)
            Ws(te, "t", s_w[par], f"s_w{par}", 16 * w_cnt[par])
            Ws(te, "t", s_g[0], "s_g0", 16 * g_cnt[0])
            for tb in range(NTB3):
                hp = tb % 2
                gelu_last = None
                for hc in range(HCE):
                    hreg = R["P0a"] if hc % 2 == 0 else R["P0b"]
                    cshift = 0 if hc % 2 == 0 else 512
                    if h_rel[hp] is not None:
                        W(se, "s", h_rel[hp][0], h_rel[hp][1])
                    pe_begin(hreg)
                    last = None
                    for ic in range(IC):
                        last = te.matmul(
                            P0[:, cshift:cshift + 512],
                            w1e[par][:, ic * 512 + hc * 128: ic * 512 + (hc + 1) * 128],
                            xnT_pe[pe][:, ic * CAP + tb * 512: ic * CAP + (tb + 1) * 512],
                            start=(ic == 0), stop=(ic == IC - 1))
                    pe_end(last, hreg)
                    rd_begin(se, "s", hreg)
                    gelu_last = cs.inc(se.activation(
                        h_sb[hp][:, hc * 512:(hc + 1) * 512], P0[:, cshift:cshift + 512],
                        AF.Gelu,
                        bias=b1c[:, e_ * 32 + we_ * 4 + hc: e_ * 32 + we_ * 4 + hc + 1]))
                    rd_end(cs, gelu_last, hreg)
                W(te, "t", cs, gelu_last)
                for tc2 in range(4):
                    tcg = tb * 4 + tc2
                    ra, rb = (R["P1a"], R["P1b"]) if tcg % 2 == 0 else (R["P2a"], R["P2b"])
                    PT = P1 if tcg % 2 == 0 else P2
                    pe_begin(ra, rb)
                    last = None
                    for hc in range(HCE):
                        for cb in range(2):
                            last = te.matmul(
                                PT[:, cb * 512:(cb + 1) * 512],
                                h_sb[hp][:, hc * 512 + tc2 * 128: hc * 512 + (tc2 + 1) * 128],
                                w2e[par][:, hc * 1024 + cb * 512: hc * 1024 + (cb + 1) * 512],
                                start=(hc == 0),
                                stop=(hc == HCE - 1 and we_ != 0))
                    if we_ == 0:
                        for cb in range(2):
                            last = te.matmul(PT[:, cb * 512:(cb + 1) * 512], ones_bf[0:1, 0:128],
                                             b2all[0:1, e_ * D + cb * 512: e_ * D + (cb + 1) * 512],
                                             start=False, stop=True)
                    pe_end(last, ra, rb)
                    rd_begin(ve, "v", ra, rb)
                    if we_ == 0:
                        Ws(ve, "v", s_t[0], "s_t0", 16 * t_cnt[0])
                        Ws(ve, "v", s_t[1], "s_t1", 16 * t_cnt[1])
                        Ws(ve, "v", s_fp, "s_fp", 16 * fp_cnt[0])
                        tok = cv.inc(ve.tensor_copy(
                            ffn_acc[:, tcg * D:(tcg + 1) * D], PT[:, :]))
                    else:
                        W(ve, "v", cv, cv.v)
                        tok = cv.inc(ve.tensor_tensor(
                            ffn_acc[:, tcg * D:(tcg + 1) * D],
                            ffn_acc[:, tcg * D:(tcg + 1) * D], PT[:, :], OP.add))
                    rd_end(cv, tok, ra, rb)
                    if we_ == NWE - 1:
                        ffn_w_tok[pe] = tok
                    pump_tr(1)
                h_rel[hp] = (ct, ct.v)
            w_rel[par] = (ct, ct.v)
            if we_ == NWE - 1:
                xnTpe_rel[pe] = ct.v
            if we_ == NWE - 1:
                W(gp, "gp", cv, ffn_w_tok[pe])
                gp.dma_start(
                    ffn_perm[e_ * CAP:(e_ + 1) * CAP, :].rearrange("(t p) c -> p t c", p=128),
                    ffn_acc[:, :].rearrange("p (t c) -> p t c", c=D)
                    ).then_inc(s_fp, 16)
                fp_cnt[0] += 1

        for p_ in range(2):
            Ws(gp, "gp", s_t[p_], f"s_t{p_}", 16 * t_cnt[p_])
            Ws(sy, "sy", s_t[p_], f"s_t{p_}", 16 * t_cnt[p_])
        Ws(gp, "gp", s_fp, "s_fp", 16 * fp_cnt[0])
        Ws(sy, "sy", s_fp, "s_fp", 16 * fp_cnt[0])
        x_rel = [None, None]
        for c in range(2 * NCH):
            par = c % 2
            if x_rel[par] is not None:
                W(sy, "sy", x_rel[par][0], x_rel[par][1])
            sy.dma_start(x_sb[par][:, :], ap["x"][c * 128:(c + 1) * 128, :]
                         ).then_inc(s_x[par], 16)
            x_cnt[par] += 1
            Ws(sy, "sy", s_o[par], f"s_o{par}", 16 * o_cnt[par])
            sy.dma_start(ybuf[par][:, :], y_tok[c * 128:(c + 1) * 128, :]
                         ).then_inc(s_r[par], 16)
            r_cnt[par] += 1
            Ws(gp, "gp", s_o[par], f"s_o{par}", 16 * o_cnt[par])
            gp.indirect_dma_start(
                fbuf[par][:, :], None,
                ffn_perm[:, :], bass.IndirectOffsetOnAxis(pos_sl[:, c:c + 1], 0),
                bounds_check=bc_reg, oob_is_err=False).then_inc(s_g[par], 16)
            g_cnt[par] += 1
            Ws(ve, "v", s_x[par], f"s_x{par}", 16 * x_cnt[par])
            Ws(ve, "v", s_r[par], f"s_r{par}", 16 * r_cnt[par])
            Ws(ve, "v", s_g[par], f"s_g{par}", 16 * g_cnt[par])
            if o_rel[par] is not None:
                W(ve, "v", o_rel[par][0], o_rel[par][1])
            Ws(ve, "v", s_o[par], f"s_o{par}", 16 * o_cnt[par])
            W(ve, "v", cv, cv.v)
            tok = cv.inc(ve.tensor_add(out_sb[par][:, :], x_sb[par][:, :], ybuf[par][:, :]))
            W(ve, "v", cv, tok)
            tok_o = cv.inc(ve.scalar_tensor_tensor(
                out_sb[par][:, :], fbuf[par][:, :], score_sl[:, c:c + 1],
                out_sb[par][:, :], OP.mult, OP.add))
            x_rel[par] = (cv, tok_o)
            o_rel[par] = (cv, tok_o)
            W(sy, "sy", cv, tok_o)
            sy.dma_start(out_d[c * 128:(c + 1) * 128, :], out_sb[par][:, :]
                         ).then_inc(s_o[par], 16)
            o_cnt[par] += 1
        ov_b.close()

        Ws(te, "t", s_o[0], "s_o0", 16 * o_cnt[0])
        Ws(te, "t", s_o[1], "s_o1", 16 * o_cnt[1])
    return nc


from concourse.bass_utils import run_bass_kernel_spmd

_NCORES = 8


def kernel(**inputs):
    nc = build(interleave=False)
    in_maps = [host_inputs(inputs, c) for c in range(_NCORES)]
    res = run_bass_kernel_spmd(nc, in_maps, core_ids=list(range(_NCORES)))
    return np.stack([np.asarray(res.results[c]["out"]) for c in range(_NCORES)], axis=0)
